# version 01
# baseline (speedup 1.0000x reference)
import numpy as np

import concourse.bass as bass
import concourse.mybir as mybir
from concourse.bass_utils import run_bass_kernel_spmd

F32 = mybir.dt.float32
I32 = mybir.dt.int32
AF = mybir.ActivationFunctionType
ALU = mybir.AluOpType
AX = mybir.AxisListType

B, L, DM = 4, 4096, 1024
DI, DS, DC = 2048, 16, 4
NE, HID = 8, 2048
TC = 512
NCH = L // TC
NIB = DI // 128
NCB = DM // 128
NQ = HID // 128
NWIN = NCH // 2
EPS = 1e-5


class Sched:
    ENGINES = ("sync", "tensor", "vector", "scalar", "gpsimd")

    def __init__(self):
        self.ops = []
        self.last_writer = {}
        self.readers = {}
        self.last_op = {}

    def add(self, engine, build, reads=(), writes=(), is_dma=False, tag="",
            group=None):
        idx = len(self.ops)
        if is_dma:
            assert group is not None, tag
        op = dict(idx=idx, engine=engine, build=build, is_dma=is_dma, tag=tag,
                  deps=set(), fdeps=set(), needs_inc=bool(is_dma), inc_val=None,
                  group=group)

        prev_v = self.last_op.get("vector")

        def needs_edge(other):
            return (self.ops[other]["engine"] != engine
                    or self.ops[other]["is_dma"] or is_dma
                    or engine == "gpsimd")

        def v_gap0(other):
            return (engine == "vector" and not is_dma and other == prev_v
                    and not self.ops[other]["is_dma"])
        for r in reads:
            w = self.last_writer.get(r)
            if w is not None:
                if needs_edge(w):
                    op["deps"].add(w)
                elif v_gap0(w):
                    op["deps"].add(w)
                    op["fdeps"].add(w)
            self.readers.setdefault(r, []).append(idx)
        for r in writes:
            w = self.last_writer.get(r)
            if w is not None and w != idx:
                if needs_edge(w):
                    op["deps"].add(w)
                elif v_gap0(w):
                    op["deps"].add(w)
                    op["fdeps"].add(w)
            for rd in self.readers.get(r, ()):
                if rd != idx and needs_edge(rd):
                    op["deps"].add(rd)
            self.last_writer[r] = idx
            self.readers[r] = []
        if not is_dma:
            self.last_op[engine] = idx
        self.ops.append(op)
        return idx

    def finalize(self, nc, block):
        for op in self.ops:
            for d in op["deps"]:
                self.ops[d]["needs_inc"] = True
        group_idxs = {}
        counters = {e: 0 for e in self.ENGINES}
        for op in self.ops:
            if op["is_dma"]:
                g = op["group"]
                group_idxs.setdefault(g, []).append(op["idx"])
                op["inc_val"] = 16 * len(group_idxs[g])
            elif op["needs_inc"]:
                counters[op["engine"]] += 1
                op["inc_val"] = counters[op["engine"]]
        sems = {}
        for e in self.ENGINES:
            if counters[e]:
                sems[e] = nc.semaphore(f"s_{e}").__enter__()
        for g in group_idxs:
            sems["g_" + g] = nc.semaphore(f"sg_{g}"[:31]).__enter__()
        self.sems = sems

        def wait_token(dep_op, consumer_idx):
            if dep_op["is_dma"]:
                g = dep_op["group"]
                n_before = sum(1 for i in group_idxs[g] if i < consumer_idx)
                return sems["g_" + g], 16 * n_before
            return sems[dep_op["engine"]], dep_op["inc_val"]

        per_engine = {e: [] for e in self.ENGINES}
        for op in self.ops:
            per_engine[op["engine"]].append(op)

        final_waits = [(sems["g_" + g], 16 * len(idxs))
                       for g, idxs in group_idxs.items()]

        def emit_stream(eng_obj, eng_name):
            memo = {}
            for op in per_engine[eng_name]:
                for d in sorted(op["deps"]):
                    dop = self.ops[d]
                    if (dop["engine"] == eng_name and not dop["is_dma"]
                            and eng_name != "gpsimd" and d not in op["fdeps"]):
                        continue
                    s, v = wait_token(dop, op["idx"])
                    assert v is not None and v > 0, (op["tag"], dop["tag"])
                    if memo.get(id(s), -1) < v:
                        eng_obj.wait_ge(s, v)
                        memo[id(s)] = v
                inst = op["build"](eng_obj)
                if op["needs_inc"]:
                    assert inst is not None, op["tag"]
                    inst.then_inc(sems["g_" + op["group"]] if op["is_dma"]
                                  else sems[eng_name],
                                  16 if op["is_dma"] else 1)
            if eng_name == "gpsimd":
                for s, v in final_waits:
                    eng_obj.wait_ge(s, v)

        for name, section in (("sync", block.sync), ("tensor", block.tensor),
                              ("vector", block.vector), ("scalar", block.scalar),
                              ("gpsimd", block.gpsimd)):
            if per_engine[name] or name == "gpsimd":
                section(lambda eng, n=name: emit_stream(eng, n))


def build_program(upto="full", dbg=False):
    nc = bass.Bass("TRN2", target_bir_lowering=False, detect_race_conditions=False)
    S = Sched()

    x_h = nc.dram_tensor("x", [L, DM], F32, kind="ExternalInput")
    win_h = nc.dram_tensor("win", [32, 128, 8, 128], F32, kind="ExternalInput")
    wout_h = nc.dram_tensor("wout", [8, 128, 16, 128], F32, kind="ExternalInput")
    xpwT_h = nc.dram_tensor("xpwT", [128, 16, 33], F32, kind="ExternalInput")
    alog_h = nc.dram_tensor("alog", [128, NIB, DS], F32, kind="ExternalInput")
    dpw_h = nc.dram_tensor("dpw", [128, NIB], F32, kind="ExternalInput")
    dpb_h = nc.dram_tensor("dpb", [128, NIB], F32, kind="ExternalInput")
    dvec_h = nc.dram_tensor("dvec", [128, NIB], F32, kind="ExternalInput")
    cw_h = nc.dram_tensor("cw", [128, NIB, DC], F32, kind="ExternalInput")
    cb_h = nc.dram_tensor("cb", [128, NIB], F32, kind="ExternalInput")
    wg_h = nc.dram_tensor("wg", [128, 8, 8], F32, kind="ExternalInput")
    w1_h = nc.dram_tensor("w1", [NE, NQ, 128, 8, 128], F32, kind="ExternalInput")
    w2_h = nc.dram_tensor("w2", [NE, NQ, 128, 8, 128], F32, kind="ExternalInput")
    w3_h = nc.dram_tensor("w3", [NE, 8, 128, 16, 128], F32, kind="ExternalInput")
    scols_h = nc.dram_tensor("scols", [1, 32], I32, kind="ExternalInput")

    out_h = nc.dram_tensor("out_part", [2048, DM], F32, kind="ExternalOutput")

    u_scr = nc.dram_tensor("u_scr", [128, NIB, L], F32, kind="Internal")
    x1_scr = nc.dram_tensor("x1_scr", [128, 20, DM], F32, kind="Internal")
    xln2_scr = nc.dram_tensor("xln2_scr", [128, 5, NCB * TC], F32, kind="Internal")
    proj_scr = nc.dram_tensor("proj_scr", [33, L], F32, kind="Internal")

    dbg_outs = {}
    if dbg:
        for nm, shp in (("d_u", [128, NIB, L]), ("d_proj", [33, L]),
                        ("d_x1", [128, 20, DM]), ("d_xln2", [128, 5, NCB * TC])):
            dbg_outs[nm] = nc.dram_tensor(nm, shp, F32, kind="ExternalOutput")

    A = nc.alloc_sbuf_tensor

    ident = A("ident", [128, 128], F32)
    a_sb = A("a_sb", [128, NIB * DS], F32)
    dpw = A("sb_dpw", [128, NIB], F32)
    dpb = A("sb_dpb", [128, NIB], F32)
    dvec = A("sb_dvec", [128, NIB], F32)
    cw = A("sb_cw", [128, NIB * DC], F32)
    cb = A("sb_cb", [128, NIB], F32)
    xpwT = A("sb_xpwT", [128, NIB * 33], F32)
    wg = A("sb_wg", [128, 8 * 8], F32)
    scols = A("sb_scols", [1, 32], I32)
    projc = A("projc", [33, 2 * TC], F32)
    carry = A("carry", [128, NIB * DS], F32)
    stat = A("stat", [128, 16], F32)
    BIGA = A("BIGA", [128, NIB * (TC + 3)], F32)
    BIGB = A("BIGB", [128, NIB * TC], F32)
    BIGC = A("BIGC", [128, NIB * TC], F32)
    lnfm = A("lnfm", [128, NCB * TC], F32)
    x1st = A("x1st", [128, 4 * DM], F32)
    attm = A("attm", [128, 4 * DM], F32)
    wstr = A("wstr", [128, 2 * 16 * 128], F32)
    ARENA = A("ARENA", [128, 8272], F32)

    ps = [nc.alloc_psum_tensor(f"ps{i}", [128, TC], F32) for i in range(8)]

    ar = ARENA.ap()
    xst = ar[:, 0:2048]
    lnst = ar[:, 2048:4096]
    abst = ar[:, 4096:5120]
    gst = ar[:, 5120:5632]
    yst = ar[:, 5632:6144]
    hst = ar[:, 6144:7168]
    atst = ar[:, 7168:8192]
    w12b = ar[:, 0:4096]
    lgt = ar[0:8, 4096:4608]
    combf = ar[0:8, 4608:5120]
    silst = ar[:, 5120:6144]
    t1st = ar[:, 6144:7168]
    outst = ar[:, 7168:8192]
    lgtm = ar[:, 8192:8224]
    gsc = ar[:, 8224:8240]
    comb = ar[:, 8240:8272]
    P1_ARENA = ["xst0", "xst1", "lnst0", "lnst1", "abst0", "abst1",
                "gst", "yst", "hst0", "hst1", "atst0", "atst1"]
    P2_ARENA = ["w12a0", "w12a1", "w12b0", "w12b1", "lgt", "combf",
                "sil0", "sil1", "t10", "t11", "outst", "lgtm", "gsc", "comb"]

    def v3(t, n, m):
        return t.ap().rearrange("p (a b) -> p a b", b=m)

    def v2(ap_, n, m):
        return ap_.rearrange("p (a b) -> p a b", b=m)

    uraw_v = v3(BIGA, NIB, TC + 3)
    ex_v = BIGA.ap()[:, 0:NIB * TC].rearrange("p (a b) -> p a b", b=TC)
    h2_v = ex_v
    u_v = v3(BIGB, NIB, TC)
    w_v = v3(BIGC, NIB, TC)
    lnfm_v = v3(lnfm, NCB, TC)
    cw_v = v3(cw, NIB, DC)
    xpwT_v = v3(xpwT, NIB, 33)
    wg_v = v3(wg, 8, 8)
    attm_v = v3(attm, 4, DM)
    x1_v = v3(x1st, 4, DM)
    xst_v = v2(xst, 2, DM)
    lnst_v = v2(lnst, 2, DM)
    ywin_v = v3(attm, NCB, TC)
    lgtm_v = v2(lgtm, 4, 8)
    comb_v = v2(comb, 4, 8)
    carry_v = v3(carry, NIB, DS)
    a_v = v3(a_sb, NIB, DS)
    w12_v = v2(w12b, 4 * 8, 128)
    bigc = BIGC.ap()
    wcol_v = v2(bigc[:, 0:2048], 16, 128)
    zst = bigc[:, 2048:3072]
    cvst = bigc[:, 3072:4096]
    BIGC_SUBS = ["wcol0", "wcol1", "zst0", "zst1", "cvst"]
    woutb_v = v2(wstr.ap(), 32, 128)
    w3s_v = v2(wstr.ap(), 32, 128)

    def sel33ap(r):
        return ident[0:33, r:r + 1].broadcast_to([33, 128])

    def sel8ap(e):
        return ident[0:8, e:e + 1].broadcast_to([8, 128])

    def op(engine, build, reads=(), writes=(), tag=""):
        S.add(engine, build, reads=reads, writes=writes, tag=tag)

    def dma_in(out, in_, writes, tag, reads=(), group=None):
        S.add("sync", lambda e, o=out, i=in_: e.dma_start(out=o, in_=i),
              reads=reads, writes=writes, is_dma=True, tag=tag,
              group=group or writes[0])

    def dma_out(out, in_, reads, tag, writes=(), group=None):
        S.add("sync", lambda e, o=out, i=in_: e.dma_start(out=o, in_=i),
              reads=reads, writes=writes, is_dma=True, tag=tag,
              group=group or tag)

    dma_in(a_sb[:], alog_h.ap().rearrange("p a b -> p (a b)"), ["a_sb"], "ld_alog", group="g_init")
    dma_in(dpw[:], dpw_h.ap(), ["dpw"], "ld_dpw", group="g_init")
    dma_in(dpb[:], dpb_h.ap(), ["dpb"], "ld_dpb", group="g_init")
    dma_in(dvec[:], dvec_h.ap(), ["dvec"], "ld_dvec", group="g_init")
    dma_in(cw[:], cw_h.ap().rearrange("p a b -> p (a b)"), ["cw"], "ld_cw", group="g_init")
    dma_in(cb[:], cb_h.ap(), ["cb"], "ld_cb", group="g_init")
    dma_in(xpwT[:], xpwT_h.ap().rearrange("p a b -> p (a b)"), ["xpwT"], "ld_xpwT", group="g_init")
    dma_in(wg[:], wg_h.ap().rearrange("p a b -> p (a b)"), ["wg"], "ld_wg", group="g_init")
    dma_in(scols[:], scols_h.ap(), ["scols"], "ld_scols", group="g_init")

    op("gpsimd", lambda e: e.memset(ident[:], 1.0), writes=["ident"], tag="id_ms")
    op("gpsimd", lambda e: e.affine_select(ident[:], ident[:], [[1, 128]],
                                           ALU.is_equal, 0.0, channel_multiplier=-1),
       reads=["ident"], writes=["ident"], tag="id_sel")
    op("scalar", lambda e: e.activation(a_sb[:], a_sb[:], AF.Exp),
       reads=["a_sb"], writes=["a_sb"], tag="a_exp")
    op("vector", lambda e: e.tensor_scalar(a_sb[:], a_sb[:], -1.0, None, ALU.mult),
       reads=["a_sb"], writes=["a_sb"], tag="a_neg")
    cr_all = [f"cr_{i}_{s}" for i in range(NIB) for s in range(DS)]
    op("vector", lambda e: e.memset(carry[:], 0.0), writes=cr_all, tag="carry_z")
    op("vector", lambda e: e.memset(stat[:, 7:8], EPS), writes=["stat"], tag="c_eps")
    op("vector", lambda e: e.memset(stat[:, 8:9], -0.5), writes=["stat"], tag="c_half")
    op("vector", lambda e: e.memset(uraw_v[:, :, 0:3], 0.0), writes=["BIGA"], tag="halo_z")

    ln_ctr = [0]

    def layernorm(src_ap, dst_ap, rsrc, rdst, par):
        c = ln_ctr[0]; ln_ctr[0] += 1
        sm = stat[:, 0:1]; mn = stat[:, 1:2]; s2 = stat[:, 2:3]
        vr = stat[:, 3:4]; rs = stat[:, 5:6]; sq = stat[:, 6:7]
        sqstage = lnst_v[:, par, :]
        rstage = f"lnst{par}"
        op("vector", lambda e: e.reduce_sum(sm, src_ap, AX.X),
           reads=[rsrc], writes=["stat"], tag=f"lnA{c}")
        op("vector", lambda e: e.tensor_scalar(mn, sm, 1.0 / DM, None, ALU.mult),
           reads=["stat"], writes=["stat"], tag=f"lnB{c}")
        op("scalar", lambda e: e.activation(sqstage, src_ap, AF.Square),
           reads=[rsrc], writes=[rstage], tag=f"lnC{c}")
        op("vector", lambda e: e.reduce_sum(s2, sqstage, AX.X),
           reads=[rstage], writes=["stat"], tag=f"lnD{c}")
        op("vector", lambda e: e.tensor_scalar(s2, s2, 1.0 / DM, None, ALU.mult),
           reads=["stat"], writes=["stat"], tag=f"lnE{c}")
        op("vector", lambda e: e.tensor_tensor(sq, mn, mn, ALU.mult),
           reads=["stat"], writes=["stat"], tag=f"lnF{c}")
        op("vector", lambda e: e.tensor_tensor(vr, s2, sq, ALU.subtract),
           reads=["stat"], writes=["stat"], tag=f"lnG{c}")
        op("scalar", lambda e: e.activation(vr, vr, AF.Ln, bias=stat[:, 7:8]),
           reads=["stat"], writes=["stat"], tag=f"lnH{c}")
        op("scalar", lambda e: e.activation(rs, vr, AF.Exp, scale=stat[:, 8:9]),
           reads=["stat"], writes=["stat"], tag=f"lnI{c}")
        op("vector", lambda e: e.tensor_scalar(dst_ap, src_ap, mn, rs,
                                               ALU.subtract, ALU.mult),
           reads=[rsrc, "stat"], writes=[rdst], tag=f"lnJ{c}")

    tr_ctr = [0]

    def transpose128(src_ap, dst_ap, rsrc, rdst, evac="vector"):
        c = tr_ctr[0]; tr_ctr[0] += 1
        bank = 6 + (c % 2)
        pr = f"psum{bank}"
        op("tensor", lambda e, bank=bank: e.transpose(ps[bank][:, 0:128], src_ap, ident[:]),
           reads=[rsrc, "ident", pr], writes=[pr], tag=f"tr{c}")
        if evac == "vector":
            op("vector", lambda e, bank=bank: e.tensor_copy(dst_ap, ps[bank][:, 0:128]),
               reads=[pr], writes=[rdst], tag=f"trev{c}")
        else:
            op("scalar", lambda e, bank=bank: e.copy(dst_ap, ps[bank][:, 0:128]),
               reads=[pr], writes=[rdst], tag=f"trev{c}")

    for j in range(NCH):
        for t in range(4):
            par = t % 2
            dma_in(xst_v[:, par, :], x_h.ap()[j * TC + t * 128: j * TC + (t + 1) * 128, :],
                   [f"xst{par}"], f"p0x{j}_{t}", reads=[f"xst{par}"])
            layernorm(xst_v[:, par, :], xst_v[:, par, :], f"xst{par}", f"xst{par}", par)
            for cbk in range(NCB):
                transpose128(xst_v[:, par, cbk * 128:(cbk + 1) * 128],
                             lnfm_v[:, cbk, t * 128:(t + 1) * 128],
                             f"xst{par}", "lnfm",
                             evac="vector" if cbk % 2 == 0 else "scalar")
        for m in range(32):
            par = m % 2
            dma_in(wcol_v[:, par * 8:(par + 1) * 8, :], win_h.ap()[m],
                   [f"wcol{par}"], f"p0w{j}_{m}", reads=[f"wcol{par}"])
            bank = m % 2
            pr = f"psum{bank}"

            def mmgrp(e, par=par, bank=bank):
                last = None
                for k in range(8):
                    last = e.matmul(ps[bank][:], wcol_v[:, par * 8 + k, :],
                                    lnfm_v[:, k, :], start=(k == 0), stop=(k == 7))
                return last
            op("tensor", mmgrp, reads=[f"wcol{par}", "lnfm", pr], writes=[pr],
               tag=f"p0mm{j}_{m}")
            if m < 16:
                op("vector", lambda e, m=m, bank=bank:
                   e.tensor_copy(uraw_v[:, m, 3:3 + TC], ps[bank][:]),
                   reads=[pr], writes=["BIGA"], tag=f"p0ue{j}_{m}")
            else:
                i = m - 16
                zp = i % 2
                op("scalar", lambda e, bank=bank, zp=zp:
                   e.activation(zst[:, zp * TC:(zp + 1) * TC], ps[bank][:], AF.Silu),
                   reads=[pr, f"zst{zp}"], writes=[f"zst{zp}"], tag=f"p0sl{j}_{i}")
                op("vector", lambda e, i=i, zp=zp:
                   e.tensor_tensor(u_v[:, i, :], u_v[:, i, :],
                                   zst[:, zp * TC:(zp + 1) * TC], ALU.mult),
                   reads=["BIGB", f"zst{zp}"], writes=["BIGB"], tag=f"p0zm{j}_{i}")
            if m == 15:
                for i in range(NIB):
                    def conv(e, i=i):
                        e.tensor_scalar(cvst[:, 0:TC], uraw_v[:, i, 0:TC],
                                        cw_v[:, i, 0:1], cb[:, i:i + 1],
                                        ALU.mult, ALU.add)
                        e.scalar_tensor_tensor(cvst[:, TC:2 * TC], uraw_v[:, i, 1:1 + TC],
                                               cw_v[:, i, 1:2], cvst[:, 0:TC],
                                               ALU.mult, ALU.add)
                        e.scalar_tensor_tensor(cvst[:, 0:TC], uraw_v[:, i, 2:2 + TC],
                                               cw_v[:, i, 2:3], cvst[:, TC:2 * TC],
                                               ALU.mult, ALU.add)
                        return e.scalar_tensor_tensor(u_v[:, i, :], uraw_v[:, i, 3:3 + TC],
                                                      cw_v[:, i, 3:4], cvst[:, 0:TC],
                                                      ALU.mult, ALU.add)
                    op("vector", conv, reads=["BIGA", "cw", "cb", "cvst", "BIGB"],
                       writes=["BIGB", "cvst"], tag=f"p0cv{j}_{i}")
        if j < NCH - 1:
            op("vector", lambda e: e.tensor_copy(uraw_v[:, :, 0:3], uraw_v[:, :, TC:TC + 3]),
               reads=["BIGA"], writes=["BIGA"], tag=f"p0hl{j}")

        def xproj(e):
            last = None
            for k in range(NIB):
                last = e.matmul(ps[2][0:33, :], xpwT_v[:, k, :], u_v[:, k, :],
                                start=(k == 0), stop=(k == NIB - 1))
            return last
        op("tensor", xproj, reads=["BIGB", "xpwT", "psum2"], writes=["psum2"],
           tag=f"p0xp{j}")
        pcp = j % 2
        op("vector", lambda e, pcp=pcp: e.tensor_copy(projc[:, pcp * TC:(pcp + 1) * TC],
                                                      ps[2][0:33, :]),
           reads=["psum2", f"projc{pcp}"], writes=[f"projc{pcp}"], tag=f"p0xe{j}")
        dma_out(proj_scr.ap()[:, j * TC:(j + 1) * TC], projc[:, pcp * TC:(pcp + 1) * TC],
                [f"projc{pcp}"], f"p0ps{j}", writes=[f"proj{j}"], group="g_pst")
        dma_out(u_scr.ap()[:, :, j * TC:(j + 1) * TC], u_v[:, :, :], ["BIGB"],
                f"p0us{j}", writes=[f"u_scr{j}"], group="g_us")

    if dbg:
        dbg_outs["d_lnfm"] = nc.dram_tensor("d_lnfm", [128, NCB * TC], F32, kind="ExternalOutput")
        dbg_outs["d_uraw"] = nc.dram_tensor("d_uraw", [128, NIB * (TC + 3)], F32, kind="ExternalOutput")
        dbg_outs["d_xst"] = nc.dram_tensor("d_xst", [128, 2 * DM], F32, kind="ExternalOutput")
        dma_out(dbg_outs["d_lnfm"].ap(), lnfm.ap(), ["lnfm"], "dbg_lnfm")
        dma_out(dbg_outs["d_uraw"].ap(), BIGA.ap(), ["BIGA"], "dbg_uraw")
        dma_out(dbg_outs["d_xst"].ap(), ARENA.ap()[:, 0:2048], ["xst0", "xst1"], "dbg_xst")
        dma_out(dbg_outs["d_u"].ap(), u_scr.ap(),
                [f"u_scr{jj}" for jj in range(NCH)], "dbg_u")
        dma_out(dbg_outs["d_proj"].ap(), proj_scr.ap(), [f"proj{jj}" for jj in range(NCH)], "dbg_proj")
    if upto == "p0":
        return nc, S, dbg_outs

    for j in range(NCH):
        dma_in(u_v[:, :, :], u_scr.ap()[:, :, j * TC:(j + 1) * TC], ["BIGB"],
               f"p1ul{j}", reads=[f"u_scr{j}"])
        pcp = j % 2
        dma_in(projc[:, pcp * TC:(pcp + 1) * TC], proj_scr.ap()[:, j * TC:(j + 1) * TC],
               [f"projc{pcp}"], f"p1pl{j}", reads=[f"proj{j}"])
        op("tensor", lambda e, pcp=pcp: e.matmul(ps[2][:], sel33ap(0),
                                             projc[:, pcp * TC:(pcp + 1) * TC],
                                             start=True, stop=True),
           reads=[f"projc{pcp}", "ident", "psum2"], writes=["psum2"], tag=f"p1dl{j}")
        for i in range(NIB):
            op("scalar", lambda e, i=i: e.activation(ex_v[:, i, :], ps[2][:], AF.Exp,
                                                     bias=dpb[:, i:i + 1],
                                                     scale=dpw[:, i:i + 1]),
               reads=["psum2", "dpw", "dpb", "BIGA"], writes=["BIGA"], tag=f"p1ex{j}_{i}")
        op("vector", lambda e: e.tensor_scalar(BIGC.ap()[:, 0:NIB * TC],
                                               BIGA.ap()[:, 0:NIB * TC],
                                               1.0, None, ALU.add),
           reads=["BIGA", "BIGC"] + (BIGC_SUBS if j == 0 else []),
           writes=["BIGC"], tag=f"p1dt{j}")
        op("vector", lambda e: e.tensor_tensor(BIGC.ap()[:, 0:NIB * TC],
                                               BIGB.ap()[:, 0:NIB * TC],
                                               BIGC.ap()[:, 0:NIB * TC], ALU.mult),
           reads=["BIGB", "BIGC"], writes=["BIGC"], tag=f"p1w{j}")
        for i in range(NIB):
            op("vector", lambda e, i=i: e.tensor_scalar(u_v[:, i, :], u_v[:, i, :],
                                                        dvec[:, i:i + 1], None, ALU.mult),
               reads=["BIGB", "dvec"], writes=["BIGB"], tag=f"p1yi{j}_{i}")
        for s in range(DS):
            bB = 3 + (s % 2) * 2
            bC = bB + 1
            prB, prC = f"psum{bB}", f"psum{bC}"
            op("tensor", lambda e, s=s, pcp=pcp, bB=bB:
               e.matmul(ps[bB][:], sel33ap(1 + s),
                        projc[:, pcp * TC:(pcp + 1) * TC], start=True, stop=True),
               reads=[f"projc{pcp}", "ident", prB], writes=[prB], tag=f"p1B{j}_{s}")
            op("tensor", lambda e, s=s, pcp=pcp, bC=bC:
               e.matmul(ps[bC][:], sel33ap(17 + s),
                        projc[:, pcp * TC:(pcp + 1) * TC], start=True, stop=True),
               reads=[f"projc{pcp}", "ident", prC], writes=[prC], tag=f"p1C{j}_{s}")
            for i in range(NIB):
                par = i % 2
                op("scalar", lambda e, i=i, s=s, par=par:
                   e.activation(abst[:, par * TC:(par + 1) * TC], ex_v[:, i, :], AF.Exp,
                                bias=a_v[:, i, s:s + 1], scale=a_v[:, i, s:s + 1]),
                   reads=["BIGA", "a_sb", f"abst{par}"], writes=[f"abst{par}"],
                   tag=f"p1ab{j}_{s}_{i}")
                op("vector", lambda e, i=i, bB=bB:
                   e.tensor_tensor(gst[:], w_v[:, i, :], ps[bB][:], ALU.mult),
                   reads=["BIGC", prB, "gst"], writes=["gst"], tag=f"p1g{j}_{s}_{i}")
                op("vector", lambda e, i=i, s=s, par=par:
                   e.tensor_tensor_scan(hst[:, par * TC:(par + 1) * TC],
                                        abst[:, par * TC:(par + 1) * TC],
                                        gst[:], carry_v[:, i, s:s + 1],
                                        ALU.mult, ALU.add),
                   reads=[f"abst{par}", "gst", f"cr_{i}_{s}", f"hst{par}"],
                   writes=[f"hst{par}"], tag=f"p1sc{j}_{s}_{i}")
                op("scalar", lambda e, i=i, s=s, par=par:
                   e.copy(carry_v[:, i, s:s + 1],
                          hst[:, par * TC + TC - 1: par * TC + TC]),
                   reads=[f"hst{par}"], writes=[f"cr_{i}_{s}"], tag=f"p1cu{j}_{s}_{i}")
                op("vector", lambda e, i=i, bC=bC, par=par:
                   e.tensor_tensor(yst[:], hst[:, par * TC:(par + 1) * TC],
                                   ps[bC][:], ALU.mult),
                   reads=[f"hst{par}", prC, "yst"], writes=["yst"], tag=f"p1yc{j}_{s}_{i}")
                op("vector", lambda e, i=i:
                   e.tensor_tensor(u_v[:, i, :], u_v[:, i, :], yst[:], ALU.add),
                   reads=["BIGB", "yst"], writes=["BIGB"], tag=f"p1ya{j}_{s}_{i}")
        for cm in range(NCB):
            par = cm % 2
            dma_in(woutb_v[:, par * 16:(par + 1) * 16, :], wout_h.ap()[cm],
                   [f"woutb{par}"], f"p1wo{j}_{cm}", reads=[f"woutb{par}"])
            bank = cm % 2
            pr = f"psum{bank}"

            def omm(e, par=par, bank=bank):
                last = None
                for k in range(NIB):
                    last = e.matmul(ps[bank][:], woutb_v[:, par * 16 + k, :],
                                    u_v[:, k, :], start=(k == 0), stop=(k == NIB - 1))
                return last
            op("tensor", omm, reads=[f"woutb{par}", "BIGB", pr], writes=[pr],
               tag=f"p1om{j}_{cm}")
            op("vector", lambda e, bank=bank, par=par:
               e.tensor_copy(atst[:, par * TC:(par + 1) * TC], ps[bank][:]),
               reads=[pr, f"atst{par}"], writes=[f"atst{par}"], tag=f"p1oe{j}_{cm}")
            for t in range(4):
                transpose128(atst[:, par * TC + t * 128: par * TC + (t + 1) * 128],
                             attm_v[:, t, cm * 128:(cm + 1) * 128],
                             f"atst{par}", "attm",
                             evac="vector" if t % 2 == 0 else "scalar")
        for t in range(4):
            par = t % 2
            dma_in(xst_v[:, par, :], x_h.ap()[j * TC + t * 128: j * TC + (t + 1) * 128, :],
                   [f"xst{par}"], f"p1x{j}_{t}", reads=[f"xst{par}"])
            op("vector", lambda e, t=t, par=par:
               e.tensor_tensor(x1_v[:, t, :], xst_v[:, par, :], attm_v[:, t, :], ALU.add),
               reads=[f"xst{par}", "attm", "x1st"], writes=["x1st"], tag=f"p1rs{j}_{t}")
            layernorm(x1_v[:, t, :], lnst_v[:, par, :], "x1st", f"lnst{par}", par)
            for cbk in range(NCB):
                transpose128(lnst_v[:, par, cbk * 128:(cbk + 1) * 128],
                             lnfm_v[:, cbk, t * 128:(t + 1) * 128],
                             f"lnst{par}", "lnfm",
                             evac="vector" if cbk % 2 == 0 else "scalar")

        def x1store(e, j=j):
            reg = e.value_load(scols[:, j:j + 1])
            inst = e.dma_start(out=x1_scr.ap()[:, bass.ts(reg, 4), :],
                               in_=x1_v[:, :, :])
            nc.free_register(reg.val)
            return inst
        S.add("gpsimd", x1store, reads=["x1st", "scols"],
              writes=[f"x1s{j % 2}_{j // 2}"], is_dma=True,
              tag=f"p1xs{j}", group="g_x1s")

        def xl2store(e, j=j):
            reg = e.value_load(scols[:, j:j + 1])
            inst = e.dma_start(
                out=xln2_scr.ap()[:, bass.ts(reg, 1), :],
                in_=lnfm.ap().rearrange("p (o f) -> p o f", o=1))
            nc.free_register(reg.val)
            return inst
        S.add("gpsimd", xl2store, reads=["lnfm", "scols"],
              writes=[f"xl2{j % 2}_{j // 2}"], is_dma=True,
              tag=f"p1x2{j}", group="g_x2s")

    if dbg:
        rd = [f"x1s{jj % 2}_{jj // 2}" for jj in range(NCH)]
        dma_out(dbg_outs["d_x1"].ap(), x1_scr.ap(), rd, "dbg_x1")
        dma_out(dbg_outs["d_xln2"].ap(), xln2_scr.ap(),
                [f"xl2{jj % 2}_{jj // 2}" for jj in range(NCH)], "dbg_xln2")
    if upto == "p1":
        return nc, S, dbg_outs

    op("vector", lambda e: e.engine_nop(), reads=P1_ARENA + ["attm"],
       writes=P2_ARENA, tag="arena_bar")
    for w in range(NWIN):
        xl_reads = [f"xl2{p}_{w}" for p in range(2)]
        x1_reads = [f"x1s{p}_{w}" for p in range(2)]
        dma_in(lnfm[:], xln2_scr.ap()[:, w, :],
               ["lnfm"], f"p2xl{w}", reads=xl_reads)
        for t in range(4):
            dma_in(x1_v[:, t, :], x1_scr.ap()[:, w * 4 + t, :],
                   ["x1st"], f"p2x1{w}_{t}", reads=x1_reads)

        def gmm(e):
            last = None
            for k in range(NCB):
                last = e.matmul(ps[7][0:8, :], wg_v[:, k, :], lnfm_v[:, k, :],
                                start=(k == 0), stop=(k == NCB - 1))
            return last
        op("tensor", gmm, reads=["lnfm", "wg", "psum7"], writes=["psum7"], tag=f"p2g{w}")
        op("vector", lambda e: e.tensor_copy(lgt, ps[7][0:8, :]),
           reads=["psum7", "lgt"], writes=["lgt"], tag=f"p2ge{w}")
        for t in range(4):
            c = tr_ctr[0]; tr_ctr[0] += 1
            bank = 6 + (c % 2)
            pr = f"psum{bank}"
            op("tensor", lambda e, t=t, bank=bank:
               e.transpose(ps[bank][:, 0:128], lgt[:, t * 128:(t + 1) * 128],
                           ident[0:8, :]),
               reads=["lgt", "ident", pr], writes=[pr], tag=f"p2lt{w}_{t}")
            op("vector", lambda e, t=t, bank=bank:
               e.tensor_copy(lgtm_v[:, t, :], ps[bank][:, 0:8]),
               reads=[pr, "lgtm"], writes=["lgtm"], tag=f"p2le{w}_{t}")
        m1 = gsc[:, 0:1]; m2 = gsc[:, 1:2]; dd = gsc[:, 2:3]
        w1c = gsc[:, 3:4]; w2c = gsc[:, 4:5]
        for t in range(4):
            lg = lgtm_v[:, t, :]
            cb_ = comb_v[:, t, :]
            op("vector", lambda e, lg=lg: e.reduce_max(m1, lg, AX.X),
               reads=["lgtm"], writes=["gsc"], tag=f"p2m1{w}_{t}")
            op("vector", lambda e, lg=lg:
               e.tensor_single_scalar(gsc[:, 5:13], lg, m1, ALU.is_ge),
               reads=["lgtm", "gsc"], writes=["gsc"], tag=f"p2k1{w}_{t}")
            op("vector", lambda e, lg=lg, cb_=cb_:
               e.scalar_tensor_tensor(cb_, gsc[:, 5:13], -1e30, lg, ALU.mult, ALU.add),
               reads=["lgtm", "gsc", "comb"], writes=["comb"], tag=f"p2l2{w}_{t}")
            op("vector", lambda e, cb_=cb_: e.reduce_max(m2, cb_, AX.X),
               reads=["comb", "gsc"], writes=["gsc"], tag=f"p2m2{w}_{t}")
            op("vector", lambda e, cb_=cb_: e.tensor_single_scalar(cb_, cb_, m2, ALU.is_ge),
               reads=["comb", "gsc"], writes=["comb"], tag=f"p2k2{w}_{t}")
            op("vector", lambda e: e.tensor_tensor(dd, m1, m2, ALU.subtract),
               reads=["gsc"], writes=["gsc"], tag=f"p2dd{w}_{t}")
            op("scalar", lambda e: e.activation(w1c, dd, AF.Sigmoid),
               reads=["gsc"], writes=["gsc"], tag=f"p2w1{w}_{t}")
            op("vector", lambda e: e.tensor_scalar(w2c, w1c, -1.0, 1.0, ALU.mult, ALU.add),
               reads=["gsc"], writes=["gsc"], tag=f"p2w2{w}_{t}")
            op("vector", lambda e, cb_=cb_: e.tensor_scalar(cb_, cb_, w2c, None, ALU.mult),
               reads=["comb", "gsc"], writes=["comb"], tag=f"p2c2{w}_{t}")
            op("vector", lambda e, cb_=cb_:
               e.scalar_tensor_tensor(cb_, gsc[:, 5:13], w1c, cb_, ALU.mult, ALU.add),
               reads=["gsc", "comb"], writes=["comb"], tag=f"p2c1{w}_{t}")
        for t in range(4):
            c = tr_ctr[0]; tr_ctr[0] += 1
            bank = 6 + (c % 2)
            pr = f"psum{bank}"
            op("tensor", lambda e, t=t, bank=bank:
               e.transpose(ps[bank][0:8, 0:128], comb_v[:, t, :], ident[:]),
               reads=["comb", "ident", pr], writes=[pr], tag=f"p2ct{w}_{t}")
            op("vector", lambda e, t=t, bank=bank:
               e.tensor_copy(combf[:, t * 128:(t + 1) * 128], ps[bank][0:8, 0:128]),
               reads=[pr, "combf"], writes=["combf"], tag=f"p2ce{w}_{t}")
        for e_i in range(NE):
            op("tensor", lambda e, e_i=e_i:
               e.matmul(ps[2][:], sel8ap(e_i), combf[:, :],
                        start=True, stop=True),
               reads=["combf", "ident", "psum2"], writes=["psum2"], tag=f"p2cr{w}_{e_i}")
            for q in range(NQ):
                par = q % 2
                dma_in(w12_v[:, par * 16:par * 16 + 8, :], w1_h.ap()[e_i, q],
                       [f"w12a{par}"], f"p2w1{w}_{e_i}_{q}", reads=[f"w12a{par}"])
                dma_in(w12_v[:, par * 16 + 8:par * 16 + 16, :], w2_h.ap()[e_i, q],
                       [f"w12b{par}"], f"p2w2{w}_{e_i}_{q}", reads=[f"w12b{par}"])

                def hgmm(e, par=par, which=0, bank=0):
                    last = None
                    for k in range(NCB):
                        last = e.matmul(ps[bank][:],
                                        w12_v[:, par * 16 + which * 8 + k, :],
                                        lnfm_v[:, k, :], start=(k == 0), stop=(k == 7))
                    return last
                op("tensor", lambda e, par=par: hgmm(e, par, 0, 0),
                   reads=[f"w12a{par}", "lnfm", "psum0"], writes=["psum0"],
                   tag=f"p2h{w}_{e_i}_{q}")
                op("tensor", lambda e, par=par: hgmm(e, par, 1, 1),
                   reads=[f"w12b{par}", "lnfm", "psum1"], writes=["psum1"],
                   tag=f"p2q{w}_{e_i}_{q}")
                op("scalar", lambda e, par=par:
                   e.activation(silst[:, par * TC:(par + 1) * TC], ps[0][:], AF.Silu),
                   reads=["psum0", f"sil{par}"], writes=[f"sil{par}"],
                   tag=f"p2s{w}_{e_i}_{q}")
                op("vector", lambda e, par=par:
                   e.tensor_tensor(t1st[:, par * TC:(par + 1) * TC],
                                   silst[:, par * TC:(par + 1) * TC], ps[1][:], ALU.mult),
                   reads=[f"sil{par}", "psum1", f"t1{par}"], writes=[f"t1{par}"],
                   tag=f"p2t{w}_{e_i}_{q}")
                op("vector", lambda e, q=q, par=par:
                   e.tensor_tensor(h2_v[:, q, :], t1st[:, par * TC:(par + 1) * TC],
                                   ps[2][:], ALU.mult),
                   reads=[f"t1{par}", "psum2", "BIGA"], writes=["BIGA"],
                   tag=f"p2u{w}_{e_i}_{q}")
            for cm in range(NCB):
                par3 = cm % 2
                bank = 3 + cm % 2
                pr = f"psum{bank}"
                dma_in(w3s_v[:, par3 * 16:(par3 + 1) * 16, :], w3_h.ap()[e_i, cm],
                       [f"w3s{par3}"], f"p2w3{w}_{e_i}_{cm}", reads=[f"w3s{par3}"])

                def ymm(e, cm=cm, bank=bank, par3=par3):
                    last = None
                    for q in range(NQ):
                        last = e.matmul(ps[bank][:], w3s_v[:, par3 * 16 + q, :],
                                        h2_v[:, q, :],
                                        start=(q == 0), stop=(q == NQ - 1))
                    return last
                op("tensor", ymm, reads=[f"w3s{par3}", "BIGA", pr], writes=[pr],
                   tag=f"p2y{w}_{e_i}_{cm}")
                if e_i == 0:
                    op("vector", lambda e, cm=cm, bank=bank:
                       e.tensor_copy(ywin_v[:, cm, :], ps[bank][:]),
                       reads=[pr, "attm"], writes=["attm"], tag=f"p2ye{w}_{e_i}_{cm}")
                else:
                    op("vector", lambda e, cm=cm, bank=bank:
                       e.tensor_tensor(ywin_v[:, cm, :], ywin_v[:, cm, :], ps[bank][:],
                                       ALU.add),
                       reads=[pr, "attm"], writes=["attm"], tag=f"p2ye{w}_{e_i}_{cm}")
        for t in range(4):
            for cm in range(NCB):
                c = tr_ctr[0]; tr_ctr[0] += 1
                bank = 6 + (c % 2)
                pr = f"psum{bank}"
                op("tensor", lambda e, t=t, cm=cm, bank=bank:
                   e.transpose(ps[bank][:, 0:128], ywin_v[:, cm, t * 128:(t + 1) * 128],
                               ident[:]),
                   reads=["attm", "ident", pr], writes=[pr], tag=f"p2ot{w}_{t}_{cm}")
                op("vector", lambda e, t=t, cm=cm, bank=bank:
                   e.tensor_tensor(outst[:, cm * 128:(cm + 1) * 128],
                                   x1_v[:, t, cm * 128:(cm + 1) * 128],
                                   ps[bank][:, 0:128], ALU.add),
                   reads=[pr, "x1st", "outst"], writes=["outst"], tag=f"p2oa{w}_{t}_{cm}")
            dma_out(out_h.ap()[(w * 4 + t) * 128:(w * 4 + t + 1) * 128, :],
                    outst, ["outst"], f"p2ou{w}_{t}", group="g_out")
    return nc, S, dbg_outs


def finalize(nc, S):
    with nc.Block() as block:
        S.finalize(nc, block)
    return nc


def _prep_weights(inp):
    f32 = lambda a: np.asarray(a, dtype=np.float32)
    w = {}
    w["win"] = np.ascontiguousarray(
        f32(inp["L0_attn_in_proj_w"]).T.reshape(8, 128, 32, 128).transpose(2, 1, 0, 3))
    w["wout"] = np.ascontiguousarray(
        f32(inp["L0_attn_out_proj_w"]).T.reshape(16, 128, 8, 128).transpose(2, 1, 0, 3))
    w["xpwT"] = np.ascontiguousarray(
        f32(inp["L0_attn_x_proj_w"]).T.reshape(16, 128, 33).transpose(1, 0, 2))
    w["alog"] = np.ascontiguousarray(
        f32(inp["L0_attn_A_log"]).reshape(16, 128, 16).transpose(1, 0, 2))
    w["dpw"] = np.ascontiguousarray(
        f32(inp["L0_attn_dt_proj_w"])[:, 0].reshape(16, 128).T)
    w["dpb"] = np.ascontiguousarray(f32(inp["L0_attn_dt_proj_b"]).reshape(16, 128).T)
    w["dvec"] = np.ascontiguousarray(f32(inp["L0_attn_D"]).reshape(16, 128).T)
    w["cw"] = np.ascontiguousarray(
        f32(inp["L0_attn_conv_w"])[:, 0, :].reshape(16, 128, 4).transpose(1, 0, 2))
    w["cb"] = np.ascontiguousarray(f32(inp["L0_attn_conv_b"]).reshape(16, 128).T)
    w["wg"] = np.ascontiguousarray(
        f32(inp["L0_ffn_Wg"]).reshape(8, 128, 8).transpose(1, 0, 2))
    w["w1"] = np.ascontiguousarray(
        f32(inp["L0_ffn_W1"]).reshape(8, 8, 128, 16, 128).transpose(0, 3, 2, 1, 4))
    w["w2"] = np.ascontiguousarray(
        f32(inp["L0_ffn_W2"]).reshape(8, 8, 128, 16, 128).transpose(0, 3, 2, 1, 4))
    w["w3"] = np.ascontiguousarray(
        f32(inp["L0_ffn_W3"]).reshape(8, 16, 128, 8, 128).transpose(0, 3, 2, 1, 4))
    return w


def _slot_arrays(parity):
    scols = np.zeros((1, 32), dtype=np.int32)
    for j in range(NCH):
        scols[0, j] = j // 2 if (j % 2 == parity) else 4
    return scols


def kernel(**inputs):
    w = _prep_weights(inputs)
    x = np.asarray(inputs["L0_attn_x"], dtype=np.float32)
    nc, S, _ = build_program(upto="full", dbg=False)
    nc = finalize(nc, S)
    in_maps = []
    for core in range(8):
        b, parity = core // 2, core % 2
        scols = _slot_arrays(parity)
        m = {"x": np.ascontiguousarray(x[b]), "srows": None, "scols": scols}
        del m["srows"]
        m.update(w)
        in_maps.append(m)
    res = run_bass_kernel_spmd(nc, in_maps, core_ids=list(range(8)))
    out = np.empty((B, L, DM), dtype=np.float32)
    for b in range(B):
        for j in range(NCH):
            core = 2 * b + (j % 2)
            sl = j // 2
            out[b, j * TC:(j + 1) * TC] = res.results[core]["out_part"][sl * TC:(sl + 1) * TC]
    return out


# version 02
# speedup vs baseline: 1.0631x; 1.0631x over previous
import numpy as np

import concourse.bass as bass
import concourse.mybir as mybir
from concourse.bass_utils import run_bass_kernel_spmd

F32 = mybir.dt.float32
I32 = mybir.dt.int32
AF = mybir.ActivationFunctionType
ALU = mybir.AluOpType
AX = mybir.AxisListType

B, L, DM = 4, 4096, 1024
DI, DS, DC = 2048, 16, 4
NE, HID = 8, 2048
TC = 512
NCH = L // TC
NIB = DI // 128
NCB = DM // 128
NQ = HID // 128
NWIN = NCH // 2
EPS = 1e-5


class Sched:
    ENGINES = ("sync", "tensor", "vector", "scalar", "gpsimd")

    def __init__(self):
        self.ops = []
        self.last_writer = {}
        self.readers = {}
        self.last_op = {}

    def add(self, engine, build, reads=(), writes=(), is_dma=False, tag="",
            group=None):
        idx = len(self.ops)
        if is_dma:
            assert group is not None, tag
        op = dict(idx=idx, engine=engine, build=build, is_dma=is_dma, tag=tag,
                  deps=set(), fdeps=set(), needs_inc=bool(is_dma), inc_val=None,
                  group=group)

        prev_v = self.last_op.get("vector")

        def needs_edge(other):
            return (self.ops[other]["engine"] != engine
                    or self.ops[other]["is_dma"] or is_dma
                    or engine == "gpsimd")

        def v_gap0(other):
            return (engine == "vector" and not is_dma and other == prev_v
                    and not self.ops[other]["is_dma"])
        for r in reads:
            w = self.last_writer.get(r)
            if w is not None:
                if needs_edge(w):
                    op["deps"].add(w)
                elif v_gap0(w):
                    op["deps"].add(w)
                    op["fdeps"].add(w)
            self.readers.setdefault(r, []).append(idx)
        for r in writes:
            w = self.last_writer.get(r)
            if w is not None and w != idx:
                if needs_edge(w):
                    op["deps"].add(w)
                elif v_gap0(w):
                    op["deps"].add(w)
                    op["fdeps"].add(w)
            for rd in self.readers.get(r, ()):
                if rd != idx and needs_edge(rd):
                    op["deps"].add(rd)
            self.last_writer[r] = idx
            self.readers[r] = []
        if not is_dma:
            self.last_op[engine] = idx
        self.ops.append(op)
        return idx

    def finalize(self, nc, block):
        for op in self.ops:
            for d in op["deps"]:
                self.ops[d]["needs_inc"] = True
        group_idxs = {}
        counters = {e: 0 for e in self.ENGINES}
        for op in self.ops:
            if op["is_dma"]:
                g = op["group"]
                group_idxs.setdefault(g, []).append(op["idx"])
                op["inc_val"] = 16 * len(group_idxs[g])
            elif op["needs_inc"]:
                counters[op["engine"]] += 1
                op["inc_val"] = counters[op["engine"]]
        sems = {}
        for e in self.ENGINES:
            if counters[e]:
                sems[e] = nc.semaphore(f"s_{e}").__enter__()
        for g in group_idxs:
            sems["g_" + g] = nc.semaphore(f"sg_{g}"[:31]).__enter__()
        self.sems = sems

        def wait_token(dep_op, consumer_idx):
            if dep_op["is_dma"]:
                g = dep_op["group"]
                n_before = sum(1 for i in group_idxs[g] if i < consumer_idx)
                return sems["g_" + g], 16 * n_before
            return sems[dep_op["engine"]], dep_op["inc_val"]

        per_engine = {e: [] for e in self.ENGINES}
        for op in self.ops:
            per_engine[op["engine"]].append(op)

        final_waits = [(sems["g_" + g], 16 * len(idxs))
                       for g, idxs in group_idxs.items()]

        def emit_stream(eng_obj, eng_name):
            memo = {}
            for op in per_engine[eng_name]:
                for d in sorted(op["deps"]):
                    dop = self.ops[d]
                    if (dop["engine"] == eng_name and not dop["is_dma"]
                            and eng_name != "gpsimd" and d not in op["fdeps"]):
                        continue
                    s, v = wait_token(dop, op["idx"])
                    assert v is not None and v > 0, (op["tag"], dop["tag"])
                    if memo.get(id(s), -1) < v:
                        eng_obj.wait_ge(s, v)
                        memo[id(s)] = v
                inst = op["build"](eng_obj)
                if op["needs_inc"]:
                    assert inst is not None, op["tag"]
                    inst.then_inc(sems["g_" + op["group"]] if op["is_dma"]
                                  else sems[eng_name],
                                  16 if op["is_dma"] else 1)
            if eng_name == "gpsimd":
                for s, v in final_waits:
                    eng_obj.wait_ge(s, v)

        for name, section in (("sync", block.sync), ("tensor", block.tensor),
                              ("vector", block.vector), ("scalar", block.scalar),
                              ("gpsimd", block.gpsimd)):
            if per_engine[name] or name == "gpsimd":
                section(lambda eng, n=name: emit_stream(eng, n))


def build_program(upto="full", dbg=False):
    nc = bass.Bass("TRN2", target_bir_lowering=False, detect_race_conditions=False)
    S = Sched()

    x_h = nc.dram_tensor("x", [L, DM], F32, kind="ExternalInput")
    win_h = nc.dram_tensor("win", [32, 128, 8, 128], F32, kind="ExternalInput")
    wout_h = nc.dram_tensor("wout", [8, 128, 16, 128], F32, kind="ExternalInput")
    xpwT_h = nc.dram_tensor("xpwT", [128, 16, 33], F32, kind="ExternalInput")
    alog_h = nc.dram_tensor("alog", [128, NIB, DS], F32, kind="ExternalInput")
    dpw_h = nc.dram_tensor("dpw", [128, NIB], F32, kind="ExternalInput")
    dpb_h = nc.dram_tensor("dpb", [128, NIB], F32, kind="ExternalInput")
    dvec_h = nc.dram_tensor("dvec", [128, NIB], F32, kind="ExternalInput")
    cw_h = nc.dram_tensor("cw", [128, NIB, DC], F32, kind="ExternalInput")
    cb_h = nc.dram_tensor("cb", [128, NIB], F32, kind="ExternalInput")
    wg_h = nc.dram_tensor("wg", [128, 8, 8], F32, kind="ExternalInput")
    w1_h = nc.dram_tensor("w1", [NE, NQ, 128, 8, 128], F32, kind="ExternalInput")
    w2_h = nc.dram_tensor("w2", [NE, NQ, 128, 8, 128], F32, kind="ExternalInput")
    w3_h = nc.dram_tensor("w3", [NE, 8, 128, 16, 128], F32, kind="ExternalInput")
    scols_h = nc.dram_tensor("scols", [1, 32], I32, kind="ExternalInput")

    out_h = nc.dram_tensor("out_part", [2048, DM], F32, kind="ExternalOutput")

    u_scr = nc.dram_tensor("u_scr", [128, NIB, L], F32, kind="Internal")
    x1_scr = nc.dram_tensor("x1_scr", [128, 20, DM], F32, kind="Internal")
    xln2_scr = nc.dram_tensor("xln2_scr", [128, 5, NCB * TC], F32, kind="Internal")
    proj_scr = nc.dram_tensor("proj_scr", [33, L], F32, kind="Internal")

    dbg_outs = {}
    if dbg:
        for nm, shp in (("d_u", [128, NIB, L]), ("d_proj", [33, L]),
                        ("d_x1", [128, 20, DM]), ("d_xln2", [128, 5, NCB * TC])):
            dbg_outs[nm] = nc.dram_tensor(nm, shp, F32, kind="ExternalOutput")

    A = nc.alloc_sbuf_tensor

    ident = A("ident", [128, 128], F32)
    a_sb = A("a_sb", [128, NIB * DS], F32)
    dpw = A("sb_dpw", [128, NIB], F32)
    dpb = A("sb_dpb", [128, NIB], F32)
    dvec = A("sb_dvec", [128, NIB], F32)
    cw = A("sb_cw", [128, NIB * DC], F32)
    cb = A("sb_cb", [128, NIB], F32)
    xpwT = A("sb_xpwT", [128, NIB * 33], F32)
    wg = A("sb_wg", [128, 8 * 8], F32)
    scols = A("sb_scols", [1, 32], I32)
    projc = A("projc", [33, 2 * TC], F32)
    carry = A("carry", [128, NIB * DS], F32)
    stat = A("stat", [128, 16], F32)
    BIGA = A("BIGA", [128, NIB * (TC + 3)], F32)
    BIGB = A("BIGB", [128, NIB * TC], F32)
    BIGC = A("BIGC", [128, NIB * TC], F32)
    lnfm = A("lnfm", [128, NCB * TC], F32)
    x1st = A("x1st", [128, 4 * DM], F32)
    attm = A("attm", [128, 4 * DM], F32)
    wstr = A("wstr", [128, 2 * 16 * 128], F32)
    ARENA = A("ARENA", [128, 9296], F32)

    ps = [nc.alloc_psum_tensor(f"ps{i}", [128, TC], F32) for i in range(8)]

    ar = ARENA.ap()
    xst = ar[:, 0:2048]
    lnst = ar[:, 2048:4096]
    abst = ar[:, 4096:5120]
    gst = ar[:, 5120:6144]
    yst = ar[:, 6144:7168]
    hst = ar[:, 7168:8192]
    atst = ar[:, 8192:9216]
    w12b = ar[:, 0:4096]
    lgt = ar[0:8, 4096:4608]
    combf = ar[0:8, 4608:5120]
    silst = ar[:, 5120:6144]
    t1st = ar[:, 6144:7168]
    outst = ar[:, 7168:8192]
    lgtm = ar[:, 9216:9248]
    gsc = ar[:, 9248:9264]
    comb = ar[:, 9264:9296]
    P1_ARENA = ["xst0", "xst1", "lnst0", "lnst1", "abst0", "abst1",
                "gst", "yst", "hst0", "hst1", "atst0", "atst1"]
    P2_ARENA = ["w12a0", "w12a1", "w12b0", "w12b1", "lgt", "combf",
                "sil0", "sil1", "t10", "t11", "outst", "lgtm", "gsc", "comb"]

    def v3(t, n, m):
        return t.ap().rearrange("p (a b) -> p a b", b=m)

    def v2(ap_, n, m):
        return ap_.rearrange("p (a b) -> p a b", b=m)

    uraw_v = v3(BIGA, NIB, TC + 3)
    ex_v = BIGA.ap()[:, 0:NIB * TC].rearrange("p (a b) -> p a b", b=TC)
    h2_v = ex_v
    u_v = v3(BIGB, NIB, TC)
    w_v = v3(BIGC, NIB, TC)
    lnfm_v = v3(lnfm, NCB, TC)
    cw_v = v3(cw, NIB, DC)
    xpwT_v = v3(xpwT, NIB, 33)
    wg_v = v3(wg, 8, 8)
    attm_v = v3(attm, 4, DM)
    x1_v = v3(x1st, 4, DM)
    xst_v = v2(xst, 2, DM)
    lnst_v = v2(lnst, 2, DM)
    ywin_v = v3(attm, NCB, TC)
    lgtm_v = v2(lgtm, 4, 8)
    comb_v = v2(comb, 4, 8)
    carry_v = v3(carry, NIB, DS)
    a_v = v3(a_sb, NIB, DS)
    w12_v = v2(w12b, 4 * 8, 128)
    bigc = BIGC.ap()
    wcol_v = v2(bigc[:, 0:2048], 16, 128)
    zst = bigc[:, 2048:3072]
    cvst = bigc[:, 3072:4096]
    BIGC_SUBS = ["wcol0", "wcol1", "zst0", "zst1", "cvst"]
    woutb_v = v2(wstr.ap(), 32, 128)
    w3s_v = v2(wstr.ap(), 32, 128)

    def sel33ap(r):
        return ident[0:33, r:r + 1].broadcast_to([33, 128])

    def sel8ap(e):
        return ident[0:8, e:e + 1].broadcast_to([8, 128])

    def op(engine, build, reads=(), writes=(), tag=""):
        S.add(engine, build, reads=reads, writes=writes, tag=tag)

    def dma_in(out, in_, writes, tag, reads=(), group=None):
        S.add("sync", lambda e, o=out, i=in_: e.dma_start(out=o, in_=i),
              reads=reads, writes=writes, is_dma=True, tag=tag,
              group=group or writes[0])

    def dma_out(out, in_, reads, tag, writes=(), group=None):
        S.add("sync", lambda e, o=out, i=in_: e.dma_start(out=o, in_=i),
              reads=reads, writes=writes, is_dma=True, tag=tag,
              group=group or tag)

    dma_in(a_sb[:], alog_h.ap().rearrange("p a b -> p (a b)"), ["a_sb"], "ld_alog", group="g_init")
    dma_in(dpw[:], dpw_h.ap(), ["dpw"], "ld_dpw", group="g_init")
    dma_in(dpb[:], dpb_h.ap(), ["dpb"], "ld_dpb", group="g_init")
    dma_in(dvec[:], dvec_h.ap(), ["dvec"], "ld_dvec", group="g_init")
    dma_in(cw[:], cw_h.ap().rearrange("p a b -> p (a b)"), ["cw"], "ld_cw", group="g_init")
    dma_in(cb[:], cb_h.ap(), ["cb"], "ld_cb", group="g_init")
    dma_in(xpwT[:], xpwT_h.ap().rearrange("p a b -> p (a b)"), ["xpwT"], "ld_xpwT", group="g_init")
    dma_in(wg[:], wg_h.ap().rearrange("p a b -> p (a b)"), ["wg"], "ld_wg", group="g_init")
    dma_in(scols[:], scols_h.ap(), ["scols"], "ld_scols", group="g_init")

    op("gpsimd", lambda e: e.memset(ident[:], 1.0), writes=["ident"], tag="id_ms")
    op("gpsimd", lambda e: e.affine_select(ident[:], ident[:], [[1, 128]],
                                           ALU.is_equal, 0.0, channel_multiplier=-1),
       reads=["ident"], writes=["ident"], tag="id_sel")
    op("scalar", lambda e: e.activation(a_sb[:], a_sb[:], AF.Exp),
       reads=["a_sb"], writes=["a_sb"], tag="a_exp")
    op("vector", lambda e: e.tensor_scalar(a_sb[:], a_sb[:], -1.0, None, ALU.mult),
       reads=["a_sb"], writes=["a_sb"], tag="a_neg")
    cr_all = [f"cr_{i}_{s}" for i in range(NIB) for s in range(DS)]
    op("vector", lambda e: e.memset(carry[:], 0.0), writes=cr_all, tag="carry_z")
    op("vector", lambda e: e.memset(stat[:, 7:8], EPS), writes=["stat"], tag="c_eps")
    op("vector", lambda e: e.memset(stat[:, 8:9], -0.5), writes=["stat"], tag="c_half")
    op("vector", lambda e: e.memset(uraw_v[:, :, 0:3], 0.0), writes=["BIGA"], tag="halo_z")

    ln_ctr = [0]

    def layernorm(src_ap, dst_ap, rsrc, rdst, par):
        c = ln_ctr[0]; ln_ctr[0] += 1
        sm = stat[:, 0:1]; mn = stat[:, 1:2]; s2 = stat[:, 2:3]
        vr = stat[:, 3:4]; rs = stat[:, 5:6]; sq = stat[:, 6:7]
        sqstage = lnst_v[:, par, :]
        rstage = f"lnst{par}"
        op("vector", lambda e: e.reduce_sum(sm, src_ap, AX.X),
           reads=[rsrc], writes=["stat"], tag=f"lnA{c}")
        op("vector", lambda e: e.tensor_scalar(mn, sm, 1.0 / DM, None, ALU.mult),
           reads=["stat"], writes=["stat"], tag=f"lnB{c}")
        op("scalar", lambda e: e.activation(sqstage, src_ap, AF.Square),
           reads=[rsrc], writes=[rstage], tag=f"lnC{c}")
        op("vector", lambda e: e.reduce_sum(s2, sqstage, AX.X),
           reads=[rstage], writes=["stat"], tag=f"lnD{c}")
        op("vector", lambda e: e.tensor_scalar(s2, s2, 1.0 / DM, None, ALU.mult),
           reads=["stat"], writes=["stat"], tag=f"lnE{c}")
        op("vector", lambda e: e.tensor_tensor(sq, mn, mn, ALU.mult),
           reads=["stat"], writes=["stat"], tag=f"lnF{c}")
        op("vector", lambda e: e.tensor_tensor(vr, s2, sq, ALU.subtract),
           reads=["stat"], writes=["stat"], tag=f"lnG{c}")
        op("scalar", lambda e: e.activation(vr, vr, AF.Ln, bias=stat[:, 7:8]),
           reads=["stat"], writes=["stat"], tag=f"lnH{c}")
        op("scalar", lambda e: e.activation(rs, vr, AF.Exp, scale=stat[:, 8:9]),
           reads=["stat"], writes=["stat"], tag=f"lnI{c}")
        op("vector", lambda e: e.tensor_scalar(dst_ap, src_ap, mn, rs,
                                               ALU.subtract, ALU.mult),
           reads=[rsrc, "stat"], writes=[rdst], tag=f"lnJ{c}")

    tr_ctr = [0]

    def transpose128(src_ap, dst_ap, rsrc, rdst, evac="vector"):
        c = tr_ctr[0]; tr_ctr[0] += 1
        bank = 6 + (c % 2)
        pr = f"psum{bank}"
        op("tensor", lambda e, bank=bank: e.transpose(ps[bank][:, 0:128], src_ap, ident[:]),
           reads=[rsrc, "ident", pr], writes=[pr], tag=f"tr{c}")
        if evac == "vector":
            op("vector", lambda e, bank=bank: e.tensor_copy(dst_ap, ps[bank][:, 0:128]),
               reads=[pr], writes=[rdst], tag=f"trev{c}")
        else:
            op("scalar", lambda e, bank=bank: e.copy(dst_ap, ps[bank][:, 0:128]),
               reads=[pr], writes=[rdst], tag=f"trev{c}")

    for j in range(NCH):
        for t in range(4):
            par = t % 2
            dma_in(xst_v[:, par, :], x_h.ap()[j * TC + t * 128: j * TC + (t + 1) * 128, :],
                   [f"xst{par}"], f"p0x{j}_{t}", reads=[f"xst{par}"])
            layernorm(xst_v[:, par, :], xst_v[:, par, :], f"xst{par}", f"xst{par}", par)
            for cbk in range(NCB):
                transpose128(xst_v[:, par, cbk * 128:(cbk + 1) * 128],
                             lnfm_v[:, cbk, t * 128:(t + 1) * 128],
                             f"xst{par}", "lnfm",
                             evac="vector" if cbk % 2 == 0 else "scalar")
        for m in range(32):
            par = m % 2
            dma_in(wcol_v[:, par * 8:(par + 1) * 8, :], win_h.ap()[m],
                   [f"wcol{par}"], f"p0w{j}_{m}", reads=[f"wcol{par}"])
            bank = m % 2
            pr = f"psum{bank}"

            def mmgrp(e, par=par, bank=bank):
                last = None
                for k in range(8):
                    last = e.matmul(ps[bank][:], wcol_v[:, par * 8 + k, :],
                                    lnfm_v[:, k, :], start=(k == 0), stop=(k == 7))
                return last
            op("tensor", mmgrp, reads=[f"wcol{par}", "lnfm", pr], writes=[pr],
               tag=f"p0mm{j}_{m}")
            if m < 16:
                op("vector", lambda e, m=m, bank=bank:
                   e.tensor_copy(uraw_v[:, m, 3:3 + TC], ps[bank][:]),
                   reads=[pr], writes=["BIGA"], tag=f"p0ue{j}_{m}")
            else:
                i = m - 16
                zp = i % 2
                op("scalar", lambda e, bank=bank, zp=zp:
                   e.activation(zst[:, zp * TC:(zp + 1) * TC], ps[bank][:], AF.Silu),
                   reads=[pr, f"zst{zp}"], writes=[f"zst{zp}"], tag=f"p0sl{j}_{i}")
                op("vector", lambda e, i=i, zp=zp:
                   e.tensor_tensor(u_v[:, i, :], u_v[:, i, :],
                                   zst[:, zp * TC:(zp + 1) * TC], ALU.mult),
                   reads=["BIGB", f"zst{zp}"], writes=["BIGB"], tag=f"p0zm{j}_{i}")
            if m == 15:
                for i in range(NIB):
                    def conv(e, i=i):
                        e.tensor_scalar(cvst[:, 0:TC], uraw_v[:, i, 0:TC],
                                        cw_v[:, i, 0:1], cb[:, i:i + 1],
                                        ALU.mult, ALU.add)
                        e.scalar_tensor_tensor(cvst[:, TC:2 * TC], uraw_v[:, i, 1:1 + TC],
                                               cw_v[:, i, 1:2], cvst[:, 0:TC],
                                               ALU.mult, ALU.add)
                        e.scalar_tensor_tensor(cvst[:, 0:TC], uraw_v[:, i, 2:2 + TC],
                                               cw_v[:, i, 2:3], cvst[:, TC:2 * TC],
                                               ALU.mult, ALU.add)
                        return e.scalar_tensor_tensor(u_v[:, i, :], uraw_v[:, i, 3:3 + TC],
                                                      cw_v[:, i, 3:4], cvst[:, 0:TC],
                                                      ALU.mult, ALU.add)
                    op("vector", conv, reads=["BIGA", "cw", "cb", "cvst", "BIGB"],
                       writes=["BIGB", "cvst"], tag=f"p0cv{j}_{i}")
        if j < NCH - 1:
            op("vector", lambda e: e.tensor_copy(uraw_v[:, :, 0:3], uraw_v[:, :, TC:TC + 3]),
               reads=["BIGA"], writes=["BIGA"], tag=f"p0hl{j}")

        def xproj(e):
            last = None
            for k in range(NIB):
                last = e.matmul(ps[2][0:33, :], xpwT_v[:, k, :], u_v[:, k, :],
                                start=(k == 0), stop=(k == NIB - 1))
            return last
        op("tensor", xproj, reads=["BIGB", "xpwT", "psum2"], writes=["psum2"],
           tag=f"p0xp{j}")
        pcp = j % 2
        op("vector", lambda e, pcp=pcp: e.tensor_copy(projc[:, pcp * TC:(pcp + 1) * TC],
                                                      ps[2][0:33, :]),
           reads=["psum2", f"projc{pcp}"], writes=[f"projc{pcp}"], tag=f"p0xe{j}")
        dma_out(proj_scr.ap()[:, j * TC:(j + 1) * TC], projc[:, pcp * TC:(pcp + 1) * TC],
                [f"projc{pcp}"], f"p0ps{j}", writes=[f"proj{j}"], group="g_pst")
        dma_out(u_scr.ap()[:, :, j * TC:(j + 1) * TC], u_v[:, :, :], ["BIGB"],
                f"p0us{j}", writes=[f"u_scr{j}"], group="g_us")

    if dbg:
        dbg_outs["d_lnfm"] = nc.dram_tensor("d_lnfm", [128, NCB * TC], F32, kind="ExternalOutput")
        dbg_outs["d_uraw"] = nc.dram_tensor("d_uraw", [128, NIB * (TC + 3)], F32, kind="ExternalOutput")
        dbg_outs["d_xst"] = nc.dram_tensor("d_xst", [128, 2 * DM], F32, kind="ExternalOutput")
        dma_out(dbg_outs["d_lnfm"].ap(), lnfm.ap(), ["lnfm"], "dbg_lnfm")
        dma_out(dbg_outs["d_uraw"].ap(), BIGA.ap(), ["BIGA"], "dbg_uraw")
        dma_out(dbg_outs["d_xst"].ap(), ARENA.ap()[:, 0:2048], ["xst0", "xst1"], "dbg_xst")
        dma_out(dbg_outs["d_u"].ap(), u_scr.ap(),
                [f"u_scr{jj}" for jj in range(NCH)], "dbg_u")
        dma_out(dbg_outs["d_proj"].ap(), proj_scr.ap(), [f"proj{jj}" for jj in range(NCH)], "dbg_proj")
    if upto == "p0":
        return nc, S, dbg_outs

    for j in range(NCH):
        dma_in(u_v[:, :, :], u_scr.ap()[:, :, j * TC:(j + 1) * TC], ["BIGB"],
               f"p1ul{j}", reads=[f"u_scr{j}"])
        pcp = j % 2
        dma_in(projc[:, pcp * TC:(pcp + 1) * TC], proj_scr.ap()[:, j * TC:(j + 1) * TC],
               [f"projc{pcp}"], f"p1pl{j}", reads=[f"proj{j}"])
        op("tensor", lambda e, pcp=pcp: e.matmul(ps[2][:], sel33ap(0),
                                             projc[:, pcp * TC:(pcp + 1) * TC],
                                             start=True, stop=True),
           reads=[f"projc{pcp}", "ident", "psum2"], writes=["psum2"], tag=f"p1dl{j}")
        for i in range(NIB):
            op("scalar", lambda e, i=i: e.activation(ex_v[:, i, :], ps[2][:], AF.Exp,
                                                     bias=dpb[:, i:i + 1],
                                                     scale=dpw[:, i:i + 1]),
               reads=["psum2", "dpw", "dpb", "BIGA"], writes=["BIGA"], tag=f"p1ex{j}_{i}")
        op("vector", lambda e: e.tensor_scalar(BIGC.ap()[:, 0:NIB * TC],
                                               BIGA.ap()[:, 0:NIB * TC],
                                               1.0, None, ALU.add),
           reads=["BIGA", "BIGC"] + (BIGC_SUBS if j == 0 else []),
           writes=["BIGC"], tag=f"p1dt{j}")
        op("vector", lambda e: e.tensor_tensor(BIGC.ap()[:, 0:NIB * TC],
                                               BIGB.ap()[:, 0:NIB * TC],
                                               BIGC.ap()[:, 0:NIB * TC], ALU.mult),
           reads=["BIGB", "BIGC"], writes=["BIGC"], tag=f"p1w{j}")
        for i in range(NIB):
            op("vector", lambda e, i=i: e.tensor_scalar(u_v[:, i, :], u_v[:, i, :],
                                                        dvec[:, i:i + 1], None, ALU.mult),
               reads=["BIGB", "dvec"], writes=["BIGB"], tag=f"p1yi{j}_{i}")
        for s in range(DS):
            bB = 3 + (s % 2) * 2
            bC = bB + 1
            prB, prC = f"psum{bB}", f"psum{bC}"
            op("tensor", lambda e, s=s, pcp=pcp, bB=bB:
               e.matmul(ps[bB][:], sel33ap(1 + s),
                        projc[:, pcp * TC:(pcp + 1) * TC], start=True, stop=True),
               reads=[f"projc{pcp}", "ident", prB], writes=[prB], tag=f"p1B{j}_{s}")
            op("tensor", lambda e, s=s, pcp=pcp, bC=bC:
               e.matmul(ps[bC][:], sel33ap(17 + s),
                        projc[:, pcp * TC:(pcp + 1) * TC], start=True, stop=True),
               reads=[f"projc{pcp}", "ident", prC], writes=[prC], tag=f"p1C{j}_{s}")
            def em_ab(i, par, s=s):
                op("scalar", lambda e, i=i, par=par, s=s:
                   e.activation(abst[:, par * TC:(par + 1) * TC], ex_v[:, i, :], AF.Exp,
                                bias=a_v[:, i, s:s + 1], scale=a_v[:, i, s:s + 1]),
                   reads=["BIGA", "a_sb", f"abst{par}"], writes=[f"abst{par}"],
                   tag=f"p1ab{j}_{s}_{i}")

            def em_g(i, par, s=s, bB=bB, prB=prB):
                op("vector", lambda e, i=i, par=par, bB=bB:
                   e.tensor_tensor(gst[:, par * TC:(par + 1) * TC],
                                   w_v[:, i, :], ps[bB][:], ALU.mult),
                   reads=["BIGC", prB, f"gst{par}"], writes=[f"gst{par}"],
                   tag=f"p1g{j}_{s}_{i}")

            def em_scan(i, par, s=s):
                op("vector", lambda e, i=i, par=par, s=s:
                   e.tensor_tensor_scan(hst[:, par * TC:(par + 1) * TC],
                                        abst[:, par * TC:(par + 1) * TC],
                                        gst[:, par * TC:(par + 1) * TC],
                                        carry_v[:, i, s:s + 1],
                                        ALU.mult, ALU.add),
                   reads=[f"abst{par}", f"gst{par}", f"cr_{i}_{s}", f"hst{par}"],
                   writes=[f"hst{par}"], tag=f"p1sc{j}_{s}_{i}")
                op("scalar", lambda e, i=i, par=par, s=s:
                   e.copy(carry_v[:, i, s:s + 1],
                          hst[:, par * TC + TC - 1: par * TC + TC]),
                   reads=[f"hst{par}"], writes=[f"cr_{i}_{s}"], tag=f"p1cu{j}_{s}_{i}")

            def em_yc(i, par, s=s, bC=bC, prC=prC):
                op("vector", lambda e, i=i, par=par, bC=bC:
                   e.tensor_tensor(yst[:, par * TC:(par + 1) * TC],
                                   hst[:, par * TC:(par + 1) * TC],
                                   ps[bC][:], ALU.mult),
                   reads=[f"hst{par}", prC, f"yst{par}"], writes=[f"yst{par}"],
                   tag=f"p1yc{j}_{s}_{i}")

            def em_ya(i, par, s=s):
                op("gpsimd", lambda e, i=i, par=par:
                   e.tensor_tensor(u_v[:, i, :], u_v[:, i, :],
                                   yst[:, par * TC:(par + 1) * TC], ALU.add),
                   reads=["BIGB", f"yst{par}"], writes=["BIGB"], tag=f"p1ya{j}_{s}_{i}")

            for i0 in range(0, NIB, 2):
                em_ab(i0, 0); em_ab(i0 + 1, 1)
                em_g(i0, 0); em_g(i0 + 1, 1)
                em_scan(i0, 0); em_scan(i0 + 1, 1)
                em_yc(i0, 0); em_yc(i0 + 1, 1)
                em_ya(i0, 0); em_ya(i0 + 1, 1)
        for cm in range(NCB):
            par = cm % 2
            dma_in(woutb_v[:, par * 16:(par + 1) * 16, :], wout_h.ap()[cm],
                   [f"woutb{par}"], f"p1wo{j}_{cm}", reads=[f"woutb{par}"])
            bank = cm % 2
            pr = f"psum{bank}"

            def omm(e, par=par, bank=bank):
                last = None
                for k in range(NIB):
                    last = e.matmul(ps[bank][:], woutb_v[:, par * 16 + k, :],
                                    u_v[:, k, :], start=(k == 0), stop=(k == NIB - 1))
                return last
            op("tensor", omm, reads=[f"woutb{par}", "BIGB", pr], writes=[pr],
               tag=f"p1om{j}_{cm}")
            op("vector", lambda e, bank=bank, par=par:
               e.tensor_copy(atst[:, par * TC:(par + 1) * TC], ps[bank][:]),
               reads=[pr, f"atst{par}"], writes=[f"atst{par}"], tag=f"p1oe{j}_{cm}")
            for t in range(4):
                transpose128(atst[:, par * TC + t * 128: par * TC + (t + 1) * 128],
                             attm_v[:, t, cm * 128:(cm + 1) * 128],
                             f"atst{par}", "attm",
                             evac="vector" if t % 2 == 0 else "scalar")
        for t in range(4):
            par = t % 2
            dma_in(xst_v[:, par, :], x_h.ap()[j * TC + t * 128: j * TC + (t + 1) * 128, :],
                   [f"xst{par}"], f"p1x{j}_{t}", reads=[f"xst{par}"])
            op("vector", lambda e, t=t, par=par:
               e.tensor_tensor(x1_v[:, t, :], xst_v[:, par, :], attm_v[:, t, :], ALU.add),
               reads=[f"xst{par}", "attm", "x1st"], writes=["x1st"], tag=f"p1rs{j}_{t}")
            layernorm(x1_v[:, t, :], lnst_v[:, par, :], "x1st", f"lnst{par}", par)
            for cbk in range(NCB):
                transpose128(lnst_v[:, par, cbk * 128:(cbk + 1) * 128],
                             lnfm_v[:, cbk, t * 128:(t + 1) * 128],
                             f"lnst{par}", "lnfm",
                             evac="vector" if cbk % 2 == 0 else "scalar")

        def x1store(e, j=j):
            reg = e.value_load(scols[:, j:j + 1])
            inst = e.dma_start(out=x1_scr.ap()[:, bass.ts(reg, 4), :],
                               in_=x1_v[:, :, :])
            nc.free_register(reg.val)
            return inst
        S.add("gpsimd", x1store, reads=["x1st", "scols"],
              writes=[f"x1s{j % 2}_{j // 2}"], is_dma=True,
              tag=f"p1xs{j}", group="g_x1s")

        def xl2store(e, j=j):
            reg = e.value_load(scols[:, j:j + 1])
            inst = e.dma_start(
                out=xln2_scr.ap()[:, bass.ts(reg, 1), :],
                in_=lnfm.ap().rearrange("p (o f) -> p o f", o=1))
            nc.free_register(reg.val)
            return inst
        S.add("gpsimd", xl2store, reads=["lnfm", "scols"],
              writes=[f"xl2{j % 2}_{j // 2}"], is_dma=True,
              tag=f"p1x2{j}", group="g_x2s")

    if dbg:
        rd = [f"x1s{jj % 2}_{jj // 2}" for jj in range(NCH)]
        dma_out(dbg_outs["d_x1"].ap(), x1_scr.ap(), rd, "dbg_x1")
        dma_out(dbg_outs["d_xln2"].ap(), xln2_scr.ap(),
                [f"xl2{jj % 2}_{jj // 2}" for jj in range(NCH)], "dbg_xln2")
    if upto == "p1":
        return nc, S, dbg_outs

    op("vector", lambda e: e.engine_nop(), reads=P1_ARENA + ["attm"],
       writes=P2_ARENA, tag="arena_bar")
    for w in range(NWIN):
        xl_reads = [f"xl2{p}_{w}" for p in range(2)]
        x1_reads = [f"x1s{p}_{w}" for p in range(2)]
        dma_in(lnfm[:], xln2_scr.ap()[:, w, :],
               ["lnfm"], f"p2xl{w}", reads=xl_reads)
        for t in range(4):
            dma_in(x1_v[:, t, :], x1_scr.ap()[:, w * 4 + t, :],
                   ["x1st"], f"p2x1{w}_{t}", reads=x1_reads)

        def gmm(e):
            last = None
            for k in range(NCB):
                last = e.matmul(ps[7][0:8, :], wg_v[:, k, :], lnfm_v[:, k, :],
                                start=(k == 0), stop=(k == NCB - 1))
            return last
        op("tensor", gmm, reads=["lnfm", "wg", "psum7"], writes=["psum7"], tag=f"p2g{w}")
        op("vector", lambda e: e.tensor_copy(lgt, ps[7][0:8, :]),
           reads=["psum7", "lgt"], writes=["lgt"], tag=f"p2ge{w}")
        for t in range(4):
            c = tr_ctr[0]; tr_ctr[0] += 1
            bank = 6 + (c % 2)
            pr = f"psum{bank}"
            op("tensor", lambda e, t=t, bank=bank:
               e.transpose(ps[bank][:, 0:128], lgt[:, t * 128:(t + 1) * 128],
                           ident[0:8, :]),
               reads=["lgt", "ident", pr], writes=[pr], tag=f"p2lt{w}_{t}")
            op("vector", lambda e, t=t, bank=bank:
               e.tensor_copy(lgtm_v[:, t, :], ps[bank][:, 0:8]),
               reads=[pr, "lgtm"], writes=["lgtm"], tag=f"p2le{w}_{t}")
        m1 = gsc[:, 0:1]; m2 = gsc[:, 1:2]; dd = gsc[:, 2:3]
        w1c = gsc[:, 3:4]; w2c = gsc[:, 4:5]
        for t in range(4):
            lg = lgtm_v[:, t, :]
            cb_ = comb_v[:, t, :]
            op("vector", lambda e, lg=lg: e.reduce_max(m1, lg, AX.X),
               reads=["lgtm"], writes=["gsc"], tag=f"p2m1{w}_{t}")
            op("vector", lambda e, lg=lg:
               e.tensor_single_scalar(gsc[:, 5:13], lg, m1, ALU.is_ge),
               reads=["lgtm", "gsc"], writes=["gsc"], tag=f"p2k1{w}_{t}")
            op("vector", lambda e, lg=lg, cb_=cb_:
               e.scalar_tensor_tensor(cb_, gsc[:, 5:13], -1e30, lg, ALU.mult, ALU.add),
               reads=["lgtm", "gsc", "comb"], writes=["comb"], tag=f"p2l2{w}_{t}")
            op("vector", lambda e, cb_=cb_: e.reduce_max(m2, cb_, AX.X),
               reads=["comb", "gsc"], writes=["gsc"], tag=f"p2m2{w}_{t}")
            op("vector", lambda e, cb_=cb_: e.tensor_single_scalar(cb_, cb_, m2, ALU.is_ge),
               reads=["comb", "gsc"], writes=["comb"], tag=f"p2k2{w}_{t}")
            op("vector", lambda e: e.tensor_tensor(dd, m1, m2, ALU.subtract),
               reads=["gsc"], writes=["gsc"], tag=f"p2dd{w}_{t}")
            op("scalar", lambda e: e.activation(w1c, dd, AF.Sigmoid),
               reads=["gsc"], writes=["gsc"], tag=f"p2w1{w}_{t}")
            op("vector", lambda e: e.tensor_scalar(w2c, w1c, -1.0, 1.0, ALU.mult, ALU.add),
               reads=["gsc"], writes=["gsc"], tag=f"p2w2{w}_{t}")
            op("vector", lambda e, cb_=cb_: e.tensor_scalar(cb_, cb_, w2c, None, ALU.mult),
               reads=["comb", "gsc"], writes=["comb"], tag=f"p2c2{w}_{t}")
            op("vector", lambda e, cb_=cb_:
               e.scalar_tensor_tensor(cb_, gsc[:, 5:13], w1c, cb_, ALU.mult, ALU.add),
               reads=["gsc", "comb"], writes=["comb"], tag=f"p2c1{w}_{t}")
        for t in range(4):
            c = tr_ctr[0]; tr_ctr[0] += 1
            bank = 6 + (c % 2)
            pr = f"psum{bank}"
            op("tensor", lambda e, t=t, bank=bank:
               e.transpose(ps[bank][0:8, 0:128], comb_v[:, t, :], ident[:]),
               reads=["comb", "ident", pr], writes=[pr], tag=f"p2ct{w}_{t}")
            op("vector", lambda e, t=t, bank=bank:
               e.tensor_copy(combf[:, t * 128:(t + 1) * 128], ps[bank][0:8, 0:128]),
               reads=[pr, "combf"], writes=["combf"], tag=f"p2ce{w}_{t}")
        for e_i in range(NE):
            op("tensor", lambda e, e_i=e_i:
               e.matmul(ps[2][:], sel8ap(e_i), combf[:, :],
                        start=True, stop=True),
               reads=["combf", "ident", "psum2"], writes=["psum2"], tag=f"p2cr{w}_{e_i}")
            for q in range(NQ):
                par = q % 2
                dma_in(w12_v[:, par * 16:par * 16 + 8, :], w1_h.ap()[e_i, q],
                       [f"w12a{par}"], f"p2w1{w}_{e_i}_{q}", reads=[f"w12a{par}"])
                dma_in(w12_v[:, par * 16 + 8:par * 16 + 16, :], w2_h.ap()[e_i, q],
                       [f"w12b{par}"], f"p2w2{w}_{e_i}_{q}", reads=[f"w12b{par}"])

                def hgmm(e, par=par, which=0, bank=0):
                    last = None
                    for k in range(NCB):
                        last = e.matmul(ps[bank][:],
                                        w12_v[:, par * 16 + which * 8 + k, :],
                                        lnfm_v[:, k, :], start=(k == 0), stop=(k == 7))
                    return last
                op("tensor", lambda e, par=par: hgmm(e, par, 0, 0),
                   reads=[f"w12a{par}", "lnfm", "psum0"], writes=["psum0"],
                   tag=f"p2h{w}_{e_i}_{q}")
                op("tensor", lambda e, par=par: hgmm(e, par, 1, 1),
                   reads=[f"w12b{par}", "lnfm", "psum1"], writes=["psum1"],
                   tag=f"p2q{w}_{e_i}_{q}")
                op("scalar", lambda e, par=par:
                   e.activation(silst[:, par * TC:(par + 1) * TC], ps[0][:], AF.Silu),
                   reads=["psum0", f"sil{par}"], writes=[f"sil{par}"],
                   tag=f"p2s{w}_{e_i}_{q}")
                op("vector", lambda e, par=par:
                   e.tensor_tensor(t1st[:, par * TC:(par + 1) * TC],
                                   silst[:, par * TC:(par + 1) * TC], ps[1][:], ALU.mult),
                   reads=[f"sil{par}", "psum1", f"t1{par}"], writes=[f"t1{par}"],
                   tag=f"p2t{w}_{e_i}_{q}")
                op("vector", lambda e, q=q, par=par:
                   e.tensor_tensor(h2_v[:, q, :], t1st[:, par * TC:(par + 1) * TC],
                                   ps[2][:], ALU.mult),
                   reads=[f"t1{par}", "psum2", "BIGA"], writes=["BIGA"],
                   tag=f"p2u{w}_{e_i}_{q}")
            for cm in range(NCB):
                par3 = cm % 2
                bank = 3 + cm % 2
                pr = f"psum{bank}"
                dma_in(w3s_v[:, par3 * 16:(par3 + 1) * 16, :], w3_h.ap()[e_i, cm],
                       [f"w3s{par3}"], f"p2w3{w}_{e_i}_{cm}", reads=[f"w3s{par3}"])

                def ymm(e, cm=cm, bank=bank, par3=par3):
                    last = None
                    for q in range(NQ):
                        last = e.matmul(ps[bank][:], w3s_v[:, par3 * 16 + q, :],
                                        h2_v[:, q, :],
                                        start=(q == 0), stop=(q == NQ - 1))
                    return last
                op("tensor", ymm, reads=[f"w3s{par3}", "BIGA", pr], writes=[pr],
                   tag=f"p2y{w}_{e_i}_{cm}")
                if e_i == 0:
                    op("vector", lambda e, cm=cm, bank=bank:
                       e.tensor_copy(ywin_v[:, cm, :], ps[bank][:]),
                       reads=[pr, "attm"], writes=["attm"], tag=f"p2ye{w}_{e_i}_{cm}")
                else:
                    op("vector", lambda e, cm=cm, bank=bank:
                       e.tensor_tensor(ywin_v[:, cm, :], ywin_v[:, cm, :], ps[bank][:],
                                       ALU.add),
                       reads=[pr, "attm"], writes=["attm"], tag=f"p2ye{w}_{e_i}_{cm}")
        for t in range(4):
            for cm in range(NCB):
                c = tr_ctr[0]; tr_ctr[0] += 1
                bank = 6 + (c % 2)
                pr = f"psum{bank}"
                op("tensor", lambda e, t=t, cm=cm, bank=bank:
                   e.transpose(ps[bank][:, 0:128], ywin_v[:, cm, t * 128:(t + 1) * 128],
                               ident[:]),
                   reads=["attm", "ident", pr], writes=[pr], tag=f"p2ot{w}_{t}_{cm}")
                op("vector", lambda e, t=t, cm=cm, bank=bank:
                   e.tensor_tensor(outst[:, cm * 128:(cm + 1) * 128],
                                   x1_v[:, t, cm * 128:(cm + 1) * 128],
                                   ps[bank][:, 0:128], ALU.add),
                   reads=[pr, "x1st", "outst"], writes=["outst"], tag=f"p2oa{w}_{t}_{cm}")
            dma_out(out_h.ap()[(w * 4 + t) * 128:(w * 4 + t + 1) * 128, :],
                    outst, ["outst"], f"p2ou{w}_{t}", group="g_out")
    return nc, S, dbg_outs


def finalize(nc, S):
    with nc.Block() as block:
        S.finalize(nc, block)
    return nc


def _prep_weights(inp):
    f32 = lambda a: np.asarray(a, dtype=np.float32)
    w = {}
    w["win"] = np.ascontiguousarray(
        f32(inp["L0_attn_in_proj_w"]).T.reshape(8, 128, 32, 128).transpose(2, 1, 0, 3))
    w["wout"] = np.ascontiguousarray(
        f32(inp["L0_attn_out_proj_w"]).T.reshape(16, 128, 8, 128).transpose(2, 1, 0, 3))
    w["xpwT"] = np.ascontiguousarray(
        f32(inp["L0_attn_x_proj_w"]).T.reshape(16, 128, 33).transpose(1, 0, 2))
    w["alog"] = np.ascontiguousarray(
        f32(inp["L0_attn_A_log"]).reshape(16, 128, 16).transpose(1, 0, 2))
    w["dpw"] = np.ascontiguousarray(
        f32(inp["L0_attn_dt_proj_w"])[:, 0].reshape(16, 128).T)
    w["dpb"] = np.ascontiguousarray(f32(inp["L0_attn_dt_proj_b"]).reshape(16, 128).T)
    w["dvec"] = np.ascontiguousarray(f32(inp["L0_attn_D"]).reshape(16, 128).T)
    w["cw"] = np.ascontiguousarray(
        f32(inp["L0_attn_conv_w"])[:, 0, :].reshape(16, 128, 4).transpose(1, 0, 2))
    w["cb"] = np.ascontiguousarray(f32(inp["L0_attn_conv_b"]).reshape(16, 128).T)
    w["wg"] = np.ascontiguousarray(
        f32(inp["L0_ffn_Wg"]).reshape(8, 128, 8).transpose(1, 0, 2))
    w["w1"] = np.ascontiguousarray(
        f32(inp["L0_ffn_W1"]).reshape(8, 8, 128, 16, 128).transpose(0, 3, 2, 1, 4))
    w["w2"] = np.ascontiguousarray(
        f32(inp["L0_ffn_W2"]).reshape(8, 8, 128, 16, 128).transpose(0, 3, 2, 1, 4))
    w["w3"] = np.ascontiguousarray(
        f32(inp["L0_ffn_W3"]).reshape(8, 16, 128, 8, 128).transpose(0, 3, 2, 1, 4))
    return w


def _slot_arrays(parity):
    scols = np.zeros((1, 32), dtype=np.int32)
    for j in range(NCH):
        scols[0, j] = j // 2 if (j % 2 == parity) else 4
    return scols


def kernel(**inputs):
    w = _prep_weights(inputs)
    x = np.asarray(inputs["L0_attn_x"], dtype=np.float32)
    nc, S, _ = build_program(upto="full", dbg=False)
    nc = finalize(nc, S)
    in_maps = []
    for core in range(8):
        b, parity = core // 2, core % 2
        scols = _slot_arrays(parity)
        m = {"x": np.ascontiguousarray(x[b]), "srows": None, "scols": scols}
        del m["srows"]
        m.update(w)
        in_maps.append(m)
    res = run_bass_kernel_spmd(nc, in_maps, core_ids=list(range(8)))
    out = np.empty((B, L, DM), dtype=np.float32)
    for b in range(B):
        for j in range(NCH):
            core = 2 * b + (j % 2)
            sl = j // 2
            out[b, j * TC:(j + 1) * TC] = res.results[core]["out_part"][sl * TC:(sl + 1) * TC]
    return out


# version 04
# speedup vs baseline: 1.5359x; 1.4447x over previous
import numpy as np

import concourse.bass as bass
import concourse.mybir as mybir
from concourse.bass_utils import run_bass_kernel_spmd

F32 = mybir.dt.float32
I32 = mybir.dt.int32
AF = mybir.ActivationFunctionType
ALU = mybir.AluOpType
AX = mybir.AxisListType

B, L, DM = 4, 4096, 1024
DI, DS, DC = 2048, 16, 4
NE, HID = 8, 2048
TC = 512
NCH = L // TC
NIB = DI // 128
NCB = DM // 128
NQ = HID // 128
NWIN = NCH // 2
EPS = 1e-5


class Sched:
    ENGINES = ("sync", "tensor", "vector", "scalar", "gpsimd")

    def __init__(self):
        self.ops = []
        self.last_writer = {}
        self.readers = {}
        self.last_op = {}

    def add(self, engine, build, reads=(), writes=(), is_dma=False, tag="",
            group=None):
        idx = len(self.ops)
        if is_dma:
            assert group is not None, tag
        op = dict(idx=idx, engine=engine, build=build, is_dma=is_dma, tag=tag,
                  deps=set(), fdeps=set(), needs_inc=bool(is_dma), inc_val=None,
                  group=group)

        prev_v = self.last_op.get("vector")

        def needs_edge(other):
            return (self.ops[other]["engine"] != engine
                    or self.ops[other]["is_dma"] or is_dma
                    or engine == "gpsimd")

        def v_gap0(other):
            return (engine == "vector" and not is_dma and other == prev_v
                    and not self.ops[other]["is_dma"])
        for r in reads:
            w = self.last_writer.get(r)
            if w is not None:
                if needs_edge(w):
                    op["deps"].add(w)
                elif v_gap0(w):
                    op["deps"].add(w)
                    op["fdeps"].add(w)
            self.readers.setdefault(r, []).append(idx)
        for r in writes:
            w = self.last_writer.get(r)
            if w is not None and w != idx:
                if needs_edge(w):
                    op["deps"].add(w)
                elif v_gap0(w):
                    op["deps"].add(w)
                    op["fdeps"].add(w)
            for rd in self.readers.get(r, ()):
                if rd != idx and needs_edge(rd):
                    op["deps"].add(rd)
            self.last_writer[r] = idx
            self.readers[r] = []
        if not is_dma:
            self.last_op[engine] = idx
        self.ops.append(op)
        return idx

    def finalize(self, nc, block):
        for op in self.ops:
            for d in op["deps"]:
                self.ops[d]["needs_inc"] = True
        group_idxs = {}
        counters = {e: 0 for e in self.ENGINES}
        for op in self.ops:
            if op["is_dma"]:
                g = op["group"]
                group_idxs.setdefault(g, []).append(op["idx"])
                op["inc_val"] = 16 * len(group_idxs[g])
            elif op["needs_inc"]:
                counters[op["engine"]] += 1
                op["inc_val"] = counters[op["engine"]]
        sems = {}
        for e in self.ENGINES:
            if counters[e]:
                sems[e] = nc.semaphore(f"s_{e}").__enter__()
        for g in group_idxs:
            sems["g_" + g] = nc.semaphore(f"sg_{g}"[:31]).__enter__()
        self.sems = sems

        def wait_token(dep_op, consumer_idx):
            if dep_op["is_dma"]:
                g = dep_op["group"]
                n_before = sum(1 for i in group_idxs[g] if i < consumer_idx)
                return sems["g_" + g], 16 * n_before
            return sems[dep_op["engine"]], dep_op["inc_val"]

        per_engine = {e: [] for e in self.ENGINES}
        for op in self.ops:
            per_engine[op["engine"]].append(op)

        final_waits = [(sems["g_" + g], 16 * len(idxs))
                       for g, idxs in group_idxs.items()]

        def emit_stream(eng_obj, eng_name):
            memo = {}
            for op in per_engine[eng_name]:
                for d in sorted(op["deps"]):
                    dop = self.ops[d]
                    if (dop["engine"] == eng_name and not dop["is_dma"]
                            and eng_name != "gpsimd" and d not in op["fdeps"]):
                        continue
                    s, v = wait_token(dop, op["idx"])
                    assert v is not None and v > 0, (op["tag"], dop["tag"])
                    if memo.get(id(s), -1) < v:
                        eng_obj.wait_ge(s, v)
                        memo[id(s)] = v
                inst = op["build"](eng_obj)
                if op["needs_inc"]:
                    assert inst is not None, op["tag"]
                    inst.then_inc(sems["g_" + op["group"]] if op["is_dma"]
                                  else sems[eng_name],
                                  16 if op["is_dma"] else 1)
            if eng_name == "gpsimd":
                for s, v in final_waits:
                    eng_obj.wait_ge(s, v)

        for name, section in (("sync", block.sync), ("tensor", block.tensor),
                              ("vector", block.vector), ("scalar", block.scalar),
                              ("gpsimd", block.gpsimd)):
            if per_engine[name] or name == "gpsimd":
                section(lambda eng, n=name: emit_stream(eng, n))


def build_program(upto="full", dbg=False):
    nc = bass.Bass("TRN2", target_bir_lowering=False, detect_race_conditions=False)
    S = Sched()

    x_h = nc.dram_tensor("x", [L, DM], F32, kind="ExternalInput")
    win_h = nc.dram_tensor("win", [32, 128, 8, 128], F32, kind="ExternalInput")
    wout_h = nc.dram_tensor("wout", [8, 128, 16, 128], F32, kind="ExternalInput")
    xpwT_h = nc.dram_tensor("xpwT", [128, 16, 33], F32, kind="ExternalInput")
    alog_h = nc.dram_tensor("alog", [128, NIB, DS], F32, kind="ExternalInput")
    dpw_h = nc.dram_tensor("dpw", [128, NIB], F32, kind="ExternalInput")
    dpb_h = nc.dram_tensor("dpb", [128, NIB], F32, kind="ExternalInput")
    dvec_h = nc.dram_tensor("dvec", [128, NIB], F32, kind="ExternalInput")
    cw_h = nc.dram_tensor("cw", [128, NIB, DC], F32, kind="ExternalInput")
    cb_h = nc.dram_tensor("cb", [128, NIB], F32, kind="ExternalInput")
    wg_h = nc.dram_tensor("wg", [128, 8, 8], F32, kind="ExternalInput")
    w1_h = nc.dram_tensor("w1", [8192, 2048], F32, kind="ExternalInput")
    w2_h = nc.dram_tensor("w2", [8192, 2048], F32, kind="ExternalInput")
    w3_h = nc.dram_tensor("w3", [8192, 2048], F32, kind="ExternalInput")
    scols_h = nc.dram_tensor("scols", [128, 32], F32, kind="ExternalInput")
    iota_h = nc.dram_tensor("iota", [128, 1], F32, kind="ExternalInput")
    tri_h = nc.dram_tensor("tri", [128, 128], F32, kind="ExternalInput")
    tstart_h = nc.dram_tensor("tstart", [1, 32], F32, kind="ExternalInput")

    out_h = nc.dram_tensor("out_part", [2048, DM], F32, kind="ExternalOutput")

    u_scr = nc.dram_tensor("u_scr", [128, NIB, L], F32, kind="Internal")
    x1_scr = nc.dram_tensor("x1_scr", [640, 4 * DM], F32, kind="Internal")
    xtm_scr = nc.dram_tensor("xtm_scr", [640, 4 * DM], F32, kind="Internal")
    lg_scr = nc.dram_tensor("lg_scr", [640, 32], F32, kind="Internal")
    proj_scr = nc.dram_tensor("proj_scr", [33, L], F32, kind="Internal")
    NDISP = 26 * 256
    NTILE = 24
    disp_scr = nc.dram_tensor("disp_scr", [NDISP, DM], F32, kind="Internal")
    ydisp_scr = nc.dram_tensor("ydisp_scr", [NDISP, DM], F32, kind="Internal")

    dbg_outs = {}
    if dbg:
        for nm, shp in (("d_u", [128, NIB, L]), ("d_proj", [33, L]),
                        ("d_x1", [128, 20, DM]), ("d_xln2", [128, 20, DM])):
            dbg_outs[nm] = nc.dram_tensor(nm, shp, F32, kind="ExternalOutput")

    A = nc.alloc_sbuf_tensor

    ident = A("ident", [128, 128], F32)
    a_sb = A("a_sb", [128, NIB * DS], F32)
    dpw = A("sb_dpw", [128, NIB], F32)
    dpb = A("sb_dpb", [128, NIB], F32)
    dvec = A("sb_dvec", [128, NIB], F32)
    cw = A("sb_cw", [128, NIB * DC], F32)
    cb = A("sb_cb", [128, NIB], F32)
    xpwT = A("sb_xpwT", [128, NIB * 33], F32)
    wg = A("sb_wg", [128, 8 * 8], F32)
    scols = A("sb_scols", [128, 32], F32)
    iotac = A("sb_iota", [128, 1], F32)
    idxc = A("sb_idx", [128, 4], F32)
    idxi = A("sb_idxi", [128, 2], I32)
    projc = A("projc", [33, 2 * TC], F32)
    carry = A("carry", [128, NIB * DS], F32)
    stat = A("stat", [128, 16], F32)
    BIGA = A("BIGA", [128, NIB * (TC + 3)], F32)
    BIGB = A("BIGB", [128, NIB * TC], F32)
    BIGC = A("BIGC", [128, NIB * TC], F32)
    lnfm = A("lnfm", [128, NCB * TC], F32)
    x1st = A("x1st", [128, 4 * DM], F32)
    attm = A("attm", [128, 4 * DM], F32)
    wstr = A("wstr", [128, 2 * 16 * 128], F32)
    tstart = A("sb_tstart", [1, 32], F32)
    onescol = A("onescol", [128, 1], F32)
    onesrow = A("onesrow", [1, 128], F32)
    ARENA = A("ARENA", [128, 9296], F32)

    ps = [nc.alloc_psum_tensor(f"ps{i}", [128, TC], F32) for i in range(8)]

    ar = ARENA.ap()
    xst = ar[:, 0:2048]
    lnst = ar[:, 2048:4096]
    abst = ar[:, 4096:5120]
    gst = ar[:, 5120:6144]
    yst = ar[:, 6144:7168]
    hst = ar[:, 7168:8192]
    atst = ar[:, 8192:9216]
    w12b = ar[:, 0:4096]
    lgt = ar[0:8, 4096:4608]
    combf = ar[0:8, 4608:5120]
    silst = ar[:, 5120:6144]
    t1st = ar[:, 6144:7168]
    outst = ar[:, 7168:8192]
    lgtm = ar[:, 9216:9248]
    gsc = ar[:, 9248:9264]
    comb = ar[:, 9264:9296]
    lgt8 = atst[0:8, 0:512]
    lgtm8 = comb
    P1_ARENA = ["xst0", "xst1", "lnst0", "lnst1", "abst0", "abst1",
                "gst", "yst", "hst0", "hst1", "atst0", "atst1", "lgtm8"]
    P2_REGIONS = ["A0", "A1", "B0", "B1", "B2", "B3", "CR",
                  "sil0", "sil1", "t10", "t11", "outst"]

    def v3(t, n, m):
        return t.ap().rearrange("p (a b) -> p a b", b=m)

    def v2(ap_, n, m):
        return ap_.rearrange("p (a b) -> p a b", b=m)

    uraw_v = v3(BIGA, NIB, TC + 3)
    ex_v = BIGA.ap()[:, 0:NIB * TC].rearrange("p (a b) -> p a b", b=TC)
    h2_v = ex_v
    u_v = v3(BIGB, NIB, TC)
    w_v = v3(BIGC, NIB, TC)
    lnfm_v = v3(lnfm, NCB, TC)
    cw_v = v3(cw, NIB, DC)
    xpwT_v = v3(xpwT, NIB, 33)
    wg_v = v3(wg, 8, 8)
    attm_v = v3(attm, 4, DM)
    x1_v = v3(x1st, 4, DM)
    xst_v = v2(xst, 2, DM)
    lnst_v = v2(lnst, 2, DM)
    ywin_v = v3(attm, NCB, TC)
    lgtm_v = v2(lgtm, 4, 8)
    comb_v = v2(comb, 4, 8)
    carry_v = v3(carry, NIB, DS)
    a_v = v3(a_sb, NIB, DS)
    w12_v = v2(w12b, 4 * 8, 128)
    bigc = BIGC.ap()
    wcol_v = v2(bigc[:, 0:2048], 16, 128)
    zst = bigc[:, 2048:3072]
    cvst = bigc[:, 3072:4096]
    BIGC_SUBS = ["wcol0", "wcol1", "zst0", "zst1", "cvst"]
    woutb_v = v2(wstr.ap(), 32, 128)
    w3s_v = v2(wstr.ap(), 32, 128)

    biga = BIGA.ap()
    bigb = BIGB.ap()
    A0 = biga[:, 0:4096]
    A1 = biga[:, 4096:8192]
    B0 = bigb[:, 0:2048]
    B1 = bigb[:, 2048:4096]
    B2 = bigb[:, 4096:6144]
    B3 = bigb[:, 6144:8192]
    CRg = bigc[:, 4096:8192]

    def sel33ap(r):
        return ident[0:33, r:r + 1].broadcast_to([33, 128])

    def sel8ap(e):
        return ident[0:8, e:e + 1].broadcast_to([8, 128])

    def op(engine, build, reads=(), writes=(), tag=""):
        S.add(engine, build, reads=reads, writes=writes, tag=tag)

    def dma_in(out, in_, writes, tag, reads=(), group=None):
        S.add("sync", lambda e, o=out, i=in_: e.dma_start(out=o, in_=i),
              reads=reads, writes=writes, is_dma=True, tag=tag,
              group=group or writes[0])

    def dma_out(out, in_, reads, tag, writes=(), group=None):
        S.add("sync", lambda e, o=out, i=in_: e.dma_start(out=o, in_=i),
              reads=reads, writes=writes, is_dma=True, tag=tag,
              group=group or tag)

    dma_in(a_sb[:], alog_h.ap().rearrange("p a b -> p (a b)"), ["a_sb"], "ld_alog", group="g_init")
    dma_in(dpw[:], dpw_h.ap(), ["dpw"], "ld_dpw", group="g_init")
    dma_in(dpb[:], dpb_h.ap(), ["dpb"], "ld_dpb", group="g_init")
    dma_in(dvec[:], dvec_h.ap(), ["dvec"], "ld_dvec", group="g_init")
    dma_in(cw[:], cw_h.ap().rearrange("p a b -> p (a b)"), ["cw"], "ld_cw", group="g_init")
    dma_in(cb[:], cb_h.ap(), ["cb"], "ld_cb", group="g_init")
    dma_in(xpwT[:], xpwT_h.ap().rearrange("p a b -> p (a b)"), ["xpwT"], "ld_xpwT", group="g_init")
    dma_in(wg[:], wg_h.ap().rearrange("p a b -> p (a b)"), ["wg"], "ld_wg", group="g_init")
    dma_in(scols[:], scols_h.ap(), ["scols"], "ld_scols", group="g_init")
    dma_in(iotac[:], iota_h.ap(), ["iota"], "ld_iota", group="g_init")

    op("gpsimd", lambda e: e.memset(ident[:], 1.0), writes=["ident"], tag="id_ms")
    op("gpsimd", lambda e: e.affine_select(ident[:], ident[:], [[1, 128]],
                                           ALU.is_equal, 0.0, channel_multiplier=-1),
       reads=["ident"], writes=["ident"], tag="id_sel")
    op("scalar", lambda e: e.activation(a_sb[:], a_sb[:], AF.Exp),
       reads=["a_sb"], writes=["a_sb"], tag="a_exp")
    op("vector", lambda e: e.tensor_scalar(a_sb[:], a_sb[:], -1.0, None, ALU.mult),
       reads=["a_sb"], writes=["a_sb"], tag="a_neg")
    cr_all = [f"cr_{i}_{s}" for i in range(NIB) for s in range(DS)]
    op("vector", lambda e: e.memset(carry[:], 0.0), writes=cr_all, tag="carry_z")
    op("vector", lambda e: e.memset(stat[:, 7:8], EPS), writes=["stat"], tag="c_eps")
    op("vector", lambda e: e.memset(onescol[:], 1.0), writes=["onescol"], tag="c_oc")
    op("vector", lambda e: e.memset(onesrow[:], 1.0), writes=["onesrow"], tag="c_or")
    op("vector", lambda e: e.memset(stat[:, 8:9], -0.5), writes=["stat"], tag="c_half")
    op("vector", lambda e: e.memset(uraw_v[:, :, 0:3], 0.0), writes=["BIGA"], tag="halo_z")

    ln_ctr = [0]

    def layernorm(src_ap, dst_ap, rsrc, rdst, par):
        c = ln_ctr[0]; ln_ctr[0] += 1
        sm = stat[:, 0:1]; mn = stat[:, 1:2]; s2 = stat[:, 2:3]
        vr = stat[:, 3:4]; rs = stat[:, 5:6]; sq = stat[:, 6:7]
        sqstage = lnst_v[:, par, :]
        rstage = f"lnst{par}"
        op("vector", lambda e: e.reduce_sum(sm, src_ap, AX.X),
           reads=[rsrc], writes=["stat"], tag=f"lnA{c}")
        op("vector", lambda e: e.tensor_scalar(mn, sm, 1.0 / DM, None, ALU.mult),
           reads=["stat"], writes=["stat"], tag=f"lnB{c}")
        op("scalar", lambda e: e.activation(sqstage, src_ap, AF.Square),
           reads=[rsrc], writes=[rstage], tag=f"lnC{c}")
        op("vector", lambda e: e.reduce_sum(s2, sqstage, AX.X),
           reads=[rstage], writes=["stat"], tag=f"lnD{c}")
        op("vector", lambda e: e.tensor_scalar(s2, s2, 1.0 / DM, None, ALU.mult),
           reads=["stat"], writes=["stat"], tag=f"lnE{c}")
        op("vector", lambda e: e.tensor_tensor(sq, mn, mn, ALU.mult),
           reads=["stat"], writes=["stat"], tag=f"lnF{c}")
        op("vector", lambda e: e.tensor_tensor(vr, s2, sq, ALU.subtract),
           reads=["stat"], writes=["stat"], tag=f"lnG{c}")
        op("scalar", lambda e: e.activation(vr, vr, AF.Ln, bias=stat[:, 7:8]),
           reads=["stat"], writes=["stat"], tag=f"lnH{c}")
        op("scalar", lambda e: e.activation(rs, vr, AF.Exp, scale=stat[:, 8:9]),
           reads=["stat"], writes=["stat"], tag=f"lnI{c}")
        op("vector", lambda e: e.tensor_scalar(dst_ap, src_ap, mn, rs,
                                               ALU.subtract, ALU.mult),
           reads=[rsrc, "stat"], writes=[rdst], tag=f"lnJ{c}")

    tr_ctr = [0]

    def transpose128(src_ap, dst_ap, rsrc, rdst, evac="vector"):
        c = tr_ctr[0]; tr_ctr[0] += 1
        bank = 6 + (c % 2)
        pr = f"psum{bank}"
        op("tensor", lambda e, bank=bank: e.transpose(ps[bank][:, 0:128], src_ap, ident[:]),
           reads=[rsrc, "ident", pr], writes=[pr], tag=f"tr{c}")
        if evac == "vector":
            op("vector", lambda e, bank=bank: e.tensor_copy(dst_ap, ps[bank][:, 0:128]),
               reads=[pr], writes=[rdst], tag=f"trev{c}")
        else:
            op("scalar", lambda e, bank=bank: e.copy(dst_ap, ps[bank][:, 0:128]),
               reads=[pr], writes=[rdst], tag=f"trev{c}")

    for j in range(NCH):
        for t in range(4):
            par = t % 2
            dma_in(xst_v[:, par, :], x_h.ap()[j * TC + t * 128: j * TC + (t + 1) * 128, :],
                   [f"xst{par}"], f"p0x{j}_{t}", reads=[f"xst{par}"])
            layernorm(xst_v[:, par, :], xst_v[:, par, :], f"xst{par}", f"xst{par}", par)
            for cbk in range(NCB):
                transpose128(xst_v[:, par, cbk * 128:(cbk + 1) * 128],
                             lnfm_v[:, cbk, t * 128:(t + 1) * 128],
                             f"xst{par}", "lnfm",
                             evac="vector" if cbk % 2 == 0 else "scalar")
        for m in range(32):
            par = m % 2
            dma_in(wcol_v[:, par * 8:(par + 1) * 8, :], win_h.ap()[m],
                   [f"wcol{par}"], f"p0w{j}_{m}", reads=[f"wcol{par}"])
            bank = m % 2
            pr = f"psum{bank}"

            def mmgrp(e, par=par, bank=bank):
                last = None
                for k in range(8):
                    last = e.matmul(ps[bank][:], wcol_v[:, par * 8 + k, :],
                                    lnfm_v[:, k, :], start=(k == 0), stop=(k == 7))
                return last
            op("tensor", mmgrp, reads=[f"wcol{par}", "lnfm", pr], writes=[pr],
               tag=f"p0mm{j}_{m}")
            if m < 16:
                op("vector", lambda e, m=m, bank=bank:
                   e.tensor_copy(uraw_v[:, m, 3:3 + TC], ps[bank][:]),
                   reads=[pr], writes=["BIGA"], tag=f"p0ue{j}_{m}")
            else:
                i = m - 16
                zp = i % 2
                op("scalar", lambda e, bank=bank, zp=zp:
                   e.activation(zst[:, zp * TC:(zp + 1) * TC], ps[bank][:], AF.Silu),
                   reads=[pr, f"zst{zp}"], writes=[f"zst{zp}"], tag=f"p0sl{j}_{i}")
                op("vector", lambda e, i=i, zp=zp:
                   e.tensor_tensor(u_v[:, i, :], u_v[:, i, :],
                                   zst[:, zp * TC:(zp + 1) * TC], ALU.mult),
                   reads=["BIGB", f"zst{zp}"], writes=["BIGB"], tag=f"p0zm{j}_{i}")
            if m == 15:
                for i in range(NIB):
                    def conv(e, i=i):
                        e.tensor_scalar(cvst[:, 0:TC], uraw_v[:, i, 0:TC],
                                        cw_v[:, i, 0:1], cb[:, i:i + 1],
                                        ALU.mult, ALU.add)
                        e.scalar_tensor_tensor(cvst[:, TC:2 * TC], uraw_v[:, i, 1:1 + TC],
                                               cw_v[:, i, 1:2], cvst[:, 0:TC],
                                               ALU.mult, ALU.add)
                        e.scalar_tensor_tensor(cvst[:, 0:TC], uraw_v[:, i, 2:2 + TC],
                                               cw_v[:, i, 2:3], cvst[:, TC:2 * TC],
                                               ALU.mult, ALU.add)
                        return e.scalar_tensor_tensor(u_v[:, i, :], uraw_v[:, i, 3:3 + TC],
                                                      cw_v[:, i, 3:4], cvst[:, 0:TC],
                                                      ALU.mult, ALU.add)
                    op("vector", conv, reads=["BIGA", "cw", "cb", "cvst", "BIGB"],
                       writes=["BIGB", "cvst"], tag=f"p0cv{j}_{i}")
        if j < NCH - 1:
            op("vector", lambda e: e.tensor_copy(uraw_v[:, :, 0:3], uraw_v[:, :, TC:TC + 3]),
               reads=["BIGA"], writes=["BIGA"], tag=f"p0hl{j}")

        def xproj(e):
            last = None
            for k in range(NIB):
                last = e.matmul(ps[2][0:33, :], xpwT_v[:, k, :], u_v[:, k, :],
                                start=(k == 0), stop=(k == NIB - 1))
            return last
        op("tensor", xproj, reads=["BIGB", "xpwT", "psum2"], writes=["psum2"],
           tag=f"p0xp{j}")
        pcp = j % 2
        op("vector", lambda e, pcp=pcp: e.tensor_copy(projc[:, pcp * TC:(pcp + 1) * TC],
                                                      ps[2][0:33, :]),
           reads=["psum2", f"projc{pcp}"], writes=[f"projc{pcp}"], tag=f"p0xe{j}")
        dma_out(proj_scr.ap()[:, j * TC:(j + 1) * TC], projc[:, pcp * TC:(pcp + 1) * TC],
                [f"projc{pcp}"], f"p0ps{j}", writes=[f"proj{j}"], group="g_pst")
        dma_out(u_scr.ap()[:, :, j * TC:(j + 1) * TC], u_v[:, :, :], ["BIGB"],
                f"p0us{j}", writes=[f"u_scr{j}"], group="g_us")

    if dbg:
        dbg_outs["d_lnfm"] = nc.dram_tensor("d_lnfm", [128, NCB * TC], F32, kind="ExternalOutput")
        dbg_outs["d_uraw"] = nc.dram_tensor("d_uraw", [128, NIB * (TC + 3)], F32, kind="ExternalOutput")
        dbg_outs["d_xst"] = nc.dram_tensor("d_xst", [128, 2 * DM], F32, kind="ExternalOutput")
        dma_out(dbg_outs["d_lnfm"].ap(), lnfm.ap(), ["lnfm"], "dbg_lnfm")
        dma_out(dbg_outs["d_uraw"].ap(), BIGA.ap(), ["BIGA"], "dbg_uraw")
        dma_out(dbg_outs["d_xst"].ap(), ARENA.ap()[:, 0:2048], ["xst0", "xst1"], "dbg_xst")
        dma_out(dbg_outs["d_u"].ap(), u_scr.ap(),
                [f"u_scr{jj}" for jj in range(NCH)], "dbg_u")
        dma_out(dbg_outs["d_proj"].ap(), proj_scr.ap(), [f"proj{jj}" for jj in range(NCH)], "dbg_proj")
    if upto == "p0":
        return nc, S, dbg_outs

    for j in range(NCH):
        dma_in(u_v[:, :, :], u_scr.ap()[:, :, j * TC:(j + 1) * TC], ["BIGB"],
               f"p1ul{j}", reads=[f"u_scr{j}"])
        pcp = j % 2
        dma_in(projc[:, pcp * TC:(pcp + 1) * TC], proj_scr.ap()[:, j * TC:(j + 1) * TC],
               [f"projc{pcp}"], f"p1pl{j}", reads=[f"proj{j}"])
        op("tensor", lambda e, pcp=pcp: e.matmul(ps[2][:], sel33ap(0),
                                             projc[:, pcp * TC:(pcp + 1) * TC],
                                             start=True, stop=True),
           reads=[f"projc{pcp}", "ident", "psum2"], writes=["psum2"], tag=f"p1dl{j}")
        for i in range(NIB):
            op("scalar", lambda e, i=i: e.activation(ex_v[:, i, :], ps[2][:], AF.Exp,
                                                     bias=dpb[:, i:i + 1],
                                                     scale=dpw[:, i:i + 1]),
               reads=["psum2", "dpw", "dpb", "BIGA"], writes=["BIGA"], tag=f"p1ex{j}_{i}")
        op("vector", lambda e: e.tensor_scalar(BIGC.ap()[:, 0:NIB * TC],
                                               BIGA.ap()[:, 0:NIB * TC],
                                               1.0, None, ALU.add),
           reads=["BIGA", "BIGC"] + (BIGC_SUBS if j == 0 else []),
           writes=["BIGC"], tag=f"p1dt{j}")
        op("vector", lambda e: e.tensor_tensor(BIGC.ap()[:, 0:NIB * TC],
                                               BIGB.ap()[:, 0:NIB * TC],
                                               BIGC.ap()[:, 0:NIB * TC], ALU.mult),
           reads=["BIGB", "BIGC"], writes=["BIGC"], tag=f"p1w{j}")
        for i in range(NIB):
            op("vector", lambda e, i=i: e.tensor_scalar(u_v[:, i, :], u_v[:, i, :],
                                                        dvec[:, i:i + 1], None, ALU.mult),
               reads=["BIGB", "dvec"], writes=["BIGB"], tag=f"p1yi{j}_{i}")
        for s in range(DS):
            bB = 3 + (s % 2) * 2
            bC = bB + 1
            prB, prC = f"psum{bB}", f"psum{bC}"
            op("tensor", lambda e, s=s, pcp=pcp, bB=bB:
               e.matmul(ps[bB][:], sel33ap(1 + s),
                        projc[:, pcp * TC:(pcp + 1) * TC], start=True, stop=True),
               reads=[f"projc{pcp}", "ident", prB], writes=[prB], tag=f"p1B{j}_{s}")
            op("tensor", lambda e, s=s, pcp=pcp, bC=bC:
               e.matmul(ps[bC][:], sel33ap(17 + s),
                        projc[:, pcp * TC:(pcp + 1) * TC], start=True, stop=True),
               reads=[f"projc{pcp}", "ident", prC], writes=[prC], tag=f"p1C{j}_{s}")
            def em_ab(i, par, s=s):
                op("scalar", lambda e, i=i, par=par, s=s:
                   e.activation(abst[:, par * TC:(par + 1) * TC], ex_v[:, i, :], AF.Exp,
                                bias=a_v[:, i, s:s + 1], scale=a_v[:, i, s:s + 1]),
                   reads=["BIGA", "a_sb", f"abst{par}"], writes=[f"abst{par}"],
                   tag=f"p1ab{j}_{s}_{i}")

            def em_g(i, par, s=s, bB=bB, prB=prB):
                op("vector", lambda e, i=i, par=par, bB=bB:
                   e.tensor_tensor(gst[:, par * TC:(par + 1) * TC],
                                   w_v[:, i, :], ps[bB][:], ALU.mult),
                   reads=["BIGC", prB, f"gst{par}"], writes=[f"gst{par}"],
                   tag=f"p1g{j}_{s}_{i}")

            def em_scan(i, par, s=s):
                op("vector", lambda e, i=i, par=par, s=s:
                   e.tensor_tensor_scan(hst[:, par * TC:(par + 1) * TC],
                                        abst[:, par * TC:(par + 1) * TC],
                                        gst[:, par * TC:(par + 1) * TC],
                                        carry_v[:, i, s:s + 1],
                                        ALU.mult, ALU.add),
                   reads=[f"abst{par}", f"gst{par}", f"cr_{i}_{s}", f"hst{par}"],
                   writes=[f"hst{par}"], tag=f"p1sc{j}_{s}_{i}")
                op("scalar", lambda e, i=i, par=par, s=s:
                   e.copy(carry_v[:, i, s:s + 1],
                          hst[:, par * TC + TC - 1: par * TC + TC]),
                   reads=[f"hst{par}"], writes=[f"cr_{i}_{s}"], tag=f"p1cu{j}_{s}_{i}")

            def em_yc(i, par, s=s, bC=bC, prC=prC):
                op("vector", lambda e, i=i, par=par, bC=bC:
                   e.tensor_tensor(yst[:, par * TC:(par + 1) * TC],
                                   hst[:, par * TC:(par + 1) * TC],
                                   ps[bC][:], ALU.mult),
                   reads=[f"hst{par}", prC, f"yst{par}"], writes=[f"yst{par}"],
                   tag=f"p1yc{j}_{s}_{i}")

            def em_ya(i, par, s=s):
                op("gpsimd", lambda e, i=i, par=par:
                   e.tensor_tensor(u_v[:, i, :], u_v[:, i, :],
                                   yst[:, par * TC:(par + 1) * TC], ALU.add),
                   reads=["BIGB", f"yst{par}"], writes=["BIGB"], tag=f"p1ya{j}_{s}_{i}")

            for i0 in range(0, NIB, 2):
                em_ab(i0, 0); em_ab(i0 + 1, 1)
                em_g(i0, 0); em_g(i0 + 1, 1)
                em_scan(i0, 0); em_scan(i0 + 1, 1)
                em_yc(i0, 0); em_yc(i0 + 1, 1)
                em_ya(i0, 0); em_ya(i0 + 1, 1)
        for cm in range(NCB):
            par = cm % 2
            dma_in(woutb_v[:, par * 16:(par + 1) * 16, :], wout_h.ap()[cm],
                   [f"woutb{par}"], f"p1wo{j}_{cm}", reads=[f"woutb{par}"])
            bank = cm % 2
            pr = f"psum{bank}"

            def omm(e, par=par, bank=bank):
                last = None
                for k in range(NIB):
                    last = e.matmul(ps[bank][:], woutb_v[:, par * 16 + k, :],
                                    u_v[:, k, :], start=(k == 0), stop=(k == NIB - 1))
                return last
            op("tensor", omm, reads=[f"woutb{par}", "BIGB", pr], writes=[pr],
               tag=f"p1om{j}_{cm}")
            op("vector", lambda e, bank=bank, par=par:
               e.tensor_copy(atst[:, par * TC:(par + 1) * TC], ps[bank][:]),
               reads=[pr, f"atst{par}"], writes=[f"atst{par}"], tag=f"p1oe{j}_{cm}")
            for t in range(4):
                transpose128(atst[:, par * TC + t * 128: par * TC + (t + 1) * 128],
                             attm_v[:, t, cm * 128:(cm + 1) * 128],
                             f"atst{par}", "attm",
                             evac="vector" if t % 2 == 0 else "scalar")
        for t in range(4):
            par = t % 2
            dma_in(xst_v[:, par, :], x_h.ap()[j * TC + t * 128: j * TC + (t + 1) * 128, :],
                   [f"xst{par}"], f"p1x{j}_{t}", reads=[f"xst{par}"])
            op("vector", lambda e, t=t, par=par:
               e.tensor_tensor(x1_v[:, t, :], xst_v[:, par, :], attm_v[:, t, :], ALU.add),
               reads=[f"xst{par}", "attm", "x1st"], writes=["x1st"], tag=f"p1rs{j}_{t}")
            layernorm(x1_v[:, t, :], attm_v[:, t, :], "x1st", "attm", par)
            for cbk in range(NCB):
                transpose128(attm_v[:, t, cbk * 128:(cbk + 1) * 128],
                             lnfm_v[:, cbk, t * 128:(t + 1) * 128],
                             "attm", "lnfm",
                             evac="vector" if cbk % 2 == 0 else "scalar")

        op("vector", lambda e, j=j:
           e.scalar_tensor_tensor(idxc[:, 0:1], scols[:, j:j + 1], 128.0,
                                  iotac[:], ALU.mult, ALU.add),
           reads=["scols", "iota", "idxc"], writes=["idxc"], tag=f"p1ix{j}")
        op("vector", lambda e: e.tensor_copy(idxi[:, 0:1], idxc[:, 0:1]),
           reads=["idxc", "idxi"], writes=["idxi"], tag=f"p1ii{j}")

        def x1store(e, j=j):
            return e.indirect_dma_start(
                out=x1_scr.ap(),
                out_offset=bass.IndirectOffsetOnAxis(idxi[:, 0:1], 0),
                in_=x1st[:], in_offset=None)
        S.add("gpsimd", x1store, reads=["x1st", "idxi"],
              writes=[f"x1s{j % 2}_{j // 2}"], is_dma=True,
              tag=f"p1xs{j}", group="g_x1s")

        def xtmstore(e, j=j):
            return e.indirect_dma_start(
                out=xtm_scr.ap(),
                out_offset=bass.IndirectOffsetOnAxis(idxi[:, 0:1], 0),
                in_=attm[:], in_offset=None)
        S.add("gpsimd", xtmstore, reads=["attm", "idxi"],
              writes=[f"xtm{j % 2}_{j // 2}"], is_dma=True,
              tag=f"p1xt{j}", group="g_xtm")

        def gmm(e):
            last = None
            for k in range(NCB):
                last = e.matmul(ps[7][0:8, :], wg_v[:, k, :], lnfm_v[:, k, :],
                                start=(k == 0), stop=(k == NCB - 1))
            return last
        op("tensor", gmm, reads=["lnfm", "wg", "psum7"], writes=["psum7"],
           tag=f"p1g{j}")
        op("vector", lambda e: e.tensor_copy(lgt8, ps[7][0:8, :]),
           reads=["psum7", "atst0"], writes=["atst0"], tag=f"p1ge{j}")
        for t in range(4):
            c = tr_ctr[0]; tr_ctr[0] += 1
            bank = 6 + (c % 2)
            pr = f"psum{bank}"
            op("tensor", lambda e, t=t, bank=bank:
               e.transpose(ps[bank][:, 0:8], lgt8[:, t * 128:(t + 1) * 128],
                           ident[0:8, 0:8]),
               reads=["atst0", "ident", pr], writes=[pr], tag=f"p1lt{j}_{t}")
            op("vector", lambda e, t=t, bank=bank:
               e.tensor_copy(lgtm8[:, t * 8:(t + 1) * 8], ps[bank][:, 0:8]),
               reads=[pr, "lgtm8"], writes=["lgtm8"], tag=f"p1le{j}_{t}")

        def lgstore(e, j=j):
            return e.indirect_dma_start(
                out=lg_scr.ap(),
                out_offset=bass.IndirectOffsetOnAxis(idxi[:, 0:1], 0),
                in_=lgtm8, in_offset=None)
        S.add("gpsimd", lgstore, reads=["lgtm8", "idxi"],
              writes=[f"lg{j % 2}_{j // 2}"], is_dma=True,
              tag=f"p1lg{j}", group="g_lg")

    if dbg:
        rd = [f"x1s{jj % 2}_{jj // 2}" for jj in range(NCH)]
        dma_out(dbg_outs["d_x1"].ap(), x1_scr.ap(), rd, "dbg_x1")
        dma_out(dbg_outs["d_xln2"].ap(), xtm_scr.ap(),
                [f"xtm{jj % 2}_{jj // 2}" for jj in range(NCH)], "dbg_xln2")
    if upto == "p1":
        return nc, S, dbg_outs

    op("vector", lambda e: e.engine_nop(), reads=P1_ARENA + ["attm", "lnfm", "x1st"],
       writes=P2_REGIONS, tag="arena_bar")

    ones32 = CRg[:, 940:972]
    wib = CRg[:, 972:996]
    wif = CRg[:, 996:1188]
    wii = CRg[:, 1188:1380].bitcast(I32)
    op("vector", lambda e: e.memset(ones32, 1.0), reads=["CR"], writes=["CR"],
       tag="p2ones")

    dma_in(wstr[:, 0:128], tri_h.ap(), ["woutb0"], "p2tri")
    dma_in(tstart[:], tstart_h.ap(), ["tstart"], "p2ts")
    tri_v = wstr.ap()[:, 0:128]

    lgall = CRg[:, 0:128].rearrange("p (a b) -> p a b", b=8)
    mx8b = CRg[:, 128:136]
    gidxb = CRg[:, 136:144].bitcast(mybir.dt.uint32)
    estr = CRg[:, 144:176]
    wtsb = CRg[:, 176:208]
    me_v = CRg[:, 208:464].rearrange("p (a b) -> p a b", b=32)
    ex_r = CRg[:, 464:720].rearrange("p (a b) -> p a b", b=32)
    cnts = CRg[:, 720:728]
    posb = CRg[:, 728:760]
    posi = CRg[:, 760:792].bitcast(I32)
    ccol = CRg[:, 792:800]
    tot = CRg[0:1, 800:808]
    padd = CRg[0:1, 808:816]
    baser = CRg[0:1, 816:824]
    scr0 = CRg[0:1, 824:856]
    etf = CRg[0:1, 856:880]
    eti = CRg[0:1, 880:904].bitcast(I32)

    for s_ in range(4):
        dma_in(lgall[:, s_ * 4:(s_ + 1) * 4, :],
               lg_scr.ap()[s_ * 128:(s_ + 1) * 128, :].rearrange("p (t e) -> p t e", e=8),
               ["CR"], f"p2lg{s_}", group="g_lgl",
               reads=[f"lg{p}_{s_}" for p in range(2)])

    for blk in range(16):
        op("vector", lambda e, blk=blk: e.max(mx8b, lgall[:, blk, :]),
           reads=["CR"], writes=["CR"], tag=f"p2mx{blk}")
        op("vector", lambda e, blk=blk: e.max_index(gidxb, mx8b, lgall[:, blk, :]),
           reads=["CR"], writes=["CR"], tag=f"p2mi{blk}")
        op("vector", lambda e, blk=blk: e.tensor_copy(estr[:, blk:blk + 1], gidxb[:, 0:1]),
           reads=["CR"], writes=["CR"], tag=f"p2e1{blk}")
        op("vector", lambda e, blk=blk:
           e.tensor_copy(estr[:, 16 + blk:17 + blk], gidxb[:, 1:2]),
           reads=["CR"], writes=["CR"], tag=f"p2e2{blk}")
        op("vector", lambda e, blk=blk:
           e.tensor_tensor(scr0[:, 0:1] if False else CRg[:, 904:905], mx8b[:, 0:1],
                           mx8b[:, 1:2], ALU.subtract),
           reads=["CR"], writes=["CR"], tag=f"p2dd{blk}")
        op("scalar", lambda e, blk=blk:
           e.activation(wtsb[:, blk:blk + 1], CRg[:, 904:905], AF.Sigmoid),
           reads=["CR"], writes=["CR"], tag=f"p2w1{blk}")
        op("vector", lambda e, blk=blk:
           e.tensor_scalar(wtsb[:, 16 + blk:17 + blk], wtsb[:, blk:blk + 1],
                           -1.0, 1.0, ALU.mult, ALU.add),
           reads=["CR"], writes=["CR"], tag=f"p2w2{blk}")

    for e_i in range(NE):
        op("vector", lambda e, e_i=e_i:
           e.tensor_single_scalar(me_v[:, e_i, :], estr, float(e_i), ALU.is_equal),
           reads=["CR"], writes=["CR"], tag=f"p2me{e_i}")
        op("vector", lambda e, e_i=e_i:
           e.tensor_tensor_scan(ex_r[:, e_i, :], ones32, me_v[:, e_i, :],
                                0.0, ALU.mult, ALU.add),
           reads=["CR"], writes=["CR"], tag=f"p2rk{e_i}")
        op("vector", lambda e, e_i=e_i:
           e.tensor_copy(cnts[:, e_i:e_i + 1], ex_r[:, e_i, 31:32]),
           reads=["CR"], writes=["CR"], tag=f"p2ct{e_i}")
        op("vector", lambda e, e_i=e_i:
           e.tensor_tensor(ex_r[:, e_i, :], ex_r[:, e_i, :], me_v[:, e_i, :],
                           ALU.subtract),
           reads=["CR"], writes=["CR"], tag=f"p2xr{e_i}")

    op("tensor", lambda e: e.matmul(ps[4][:, 0:8], tri_v, cnts, start=True, stop=True),
       reads=["woutb0", "CR", "psum4"], writes=["psum4"], tag="p2pf")
    op("tensor", lambda e: e.matmul(ps[5][0:1, 0:8], onescol[:], cnts, start=True, stop=True),
       reads=["onescol", "CR", "psum5"], writes=["psum5"], tag="p2tt")
    op("vector", lambda e: e.tensor_copy(tot, ps[5][0:1, 0:8]),
       reads=["psum5", "CR"], writes=["CR"], tag="p2te")
    op("vector", lambda e: e.memset(padd, 0.0), reads=["CR"], writes=["CR"], tag="p2pz")
    for kk in range(16):
        op("vector", lambda e, kk=kk:
           e.tensor_single_scalar(scr0[:, 0:8], tot, float(256 * kk), ALU.is_gt),
           reads=["CR"], writes=["CR"], tag=f"p2pa{kk}")
        op("vector", lambda e, kk=kk:
           e.tensor_tensor(padd, padd, scr0[:, 0:8], ALU.add),
           reads=["CR"], writes=["CR"], tag=f"p2pb{kk}")
    op("vector", lambda e: e.tensor_scalar(padd, padd, 256.0, None, ALU.mult),
       reads=["CR"], writes=["CR"], tag="p2pm")
    op("vector", lambda e:
       e.tensor_tensor_scan(baser, ones32[0:1, 0:8], padd, 0.0, ALU.mult, ALU.add),
       reads=["CR"], writes=["CR"], tag="p2bs")
    op("vector", lambda e: e.tensor_tensor(baser, baser, padd, ALU.subtract),
       reads=["CR"], writes=["CR"], tag="p2bx")
    op("tensor", lambda e: e.matmul(ps[5][:, 0:8], onesrow[:], baser, start=True, stop=True),
       reads=["onesrow", "CR", "psum5"], writes=["psum5"], tag="p2br")
    op("vector", lambda e: e.tensor_copy(ccol, ps[4][:, 0:8]),
       reads=["psum4", "CR"], writes=["CR"], tag="p2cp")
    op("vector", lambda e: e.tensor_tensor(ccol, ccol, ps[5][:, 0:8], ALU.add),
       reads=["psum5", "CR"], writes=["CR"], tag="p2cc")
    op("vector", lambda e: e.memset(posb, 0.0), reads=["CR"], writes=["CR"], tag="p2po0")
    for e_i in range(NE):
        op("vector", lambda e, e_i=e_i:
           e.scalar_tensor_tensor(CRg[:, 908:940], ex_r[:, e_i, :],
                                  ccol[:, e_i:e_i + 1], me_v[:, e_i, :],
                                  ALU.add, ALU.mult),
           reads=["CR"], writes=["CR"], tag=f"p2pp{e_i}")
        op("vector", lambda e, e_i=e_i:
           e.tensor_tensor(posb, posb, CRg[:, 908:940], ALU.add),
           reads=["CR"], writes=["CR"], tag=f"p2pq{e_i}")
    op("vector", lambda e: e.tensor_copy(posi, posb),
       reads=["CR"], writes=["CR"], tag="p2pi")
    op("vector", lambda e: e.memset(etf, 0.0), reads=["CR"], writes=["CR"], tag="p2ez")
    for e_i in range(NE):
        op("vector", lambda e, e_i=e_i:
           e.tensor_single_scalar(scr0[:, 8:32], tstart[:, 0:24],
                                  baser[:, e_i:e_i + 1], ALU.is_ge),
           reads=["CR", "tstart"], writes=["CR"], tag=f"p2ea{e_i}")
        op("vector", lambda e, e_i=e_i:
           e.tensor_tensor(etf, etf, scr0[:, 8:32], ALU.add),
           reads=["CR"], writes=["CR"], tag=f"p2eb{e_i}")
    op("vector", lambda e: e.tensor_scalar(etf, etf, 1.0, None, ALU.subtract),
       reads=["CR"], writes=["CR"], tag="p2em")
    op("tensor", lambda e: e.matmul(ps[5][:, 0:24], onesrow[:], etf, start=True, stop=True),
       reads=["onesrow", "CR", "psum5"], writes=["psum5"], tag="p2er")
    op("vector", lambda e:
       e.scalar_tensor_tensor(wib, ps[5][:, 0:24], 1024.0,
                              iotac[:].broadcast_to([128, 24]), ALU.mult, ALU.add),
       reads=["psum5", "iota", "CR"], writes=["CR"], tag="p2wb")
    for ph in range(8):
        op("vector", lambda e, ph=ph:
           e.tensor_scalar(wif[:, ph * 24:(ph + 1) * 24], wib, float(ph * 128),
                           None, ALU.add),
           reads=["CR"], writes=["CR"], tag=f"p2wi{ph}")
    op("vector", lambda e: e.tensor_copy(wii, wif),
       reads=["CR"], writes=["CR"], tag="p2wc")

    op("vector", lambda e: e.memset(attm_v[:, :, :], 0.0),
       reads=["attm"], writes=["attm"], tag="p2z")
    for r in range(13):
        dma_out(disp_scr.ap()[r * 512:(r + 1) * 512, :].rearrange(
                    "(a p) m -> p a m", p=128),
                attm_v[:, :, :], ["attm"], f"p2zc{r}", writes=["disp"], group="g_zc")

    for a in range(2):
        for blk in range(16):
            dma_in(B0[:, 0:1024],
                   xtm_scr.ap()[(blk // 4) * 128:(blk // 4 + 1) * 128,
                                (blk % 4) * 1024:(blk % 4 + 1) * 1024],
                   ["B0"], f"p2sl{a}_{blk}",
                   reads=[f"xtm{p}_{blk // 4}" for p in range(2)])

            def scat(e, a=a, blk=blk):
                return e.indirect_dma_start(
                    out=disp_scr.ap(),
                    out_offset=bass.IndirectOffsetOnAxis(posi[:, a * 16 + blk: a * 16 + blk + 1], 0),
                    in_=B0[:, 0:1024], in_offset=None)
            S.add("gpsimd", scat, reads=["B0", "CR", "disp"], writes=["disp"],
                  is_dma=True, tag=f"p2sc{a}_{blk}", group="g_sc")

    w12a_v = biga[:, 0:4096].rearrange("p (d k m) -> p d k m", d=4, k=8)
    w12b_vv = biga[:, 4096:8192].rearrange("p (d k m) -> p d k m", d=4, k=8)
    w3v_a = bigb[:, 4096:6144].rearrange("p (d m) -> p d m", d=2)
    w3v_b = bigb[:, 6144:8192].rearrange("p (d m) -> p d m", d=2)
    xtm_v = B0.rearrange("p (a m) -> p a m", m=1024)
    xfm_v = B1.rearrange("p (k t) -> p k t", t=256)
    h2_vv = bigb[:, 1024:1536].rearrange("p (d t) -> p d t", t=256)
    sil_v = silst
    for k_t in range(NTILE):
        dma_in(xtm_v[:, :, :], disp_scr.ap()[k_t * 256:(k_t + 1) * 256, :].rearrange(
                   "(a p) m -> p a m", p=128), ["B0"], f"p2tl{k_t}", reads=["disp"])
        for a in range(2):
            for cbk in range(NCB):
                transpose128(xtm_v[:, a, cbk * 128:(cbk + 1) * 128],
                             xfm_v[:, cbk, a * 128:(a + 1) * 128],
                             "B0", "B1",
                             evac="vector" if cbk % 2 == 0 else "scalar")
        for ph in range(8):
            wpar = ph % 2
            wbuf = w12a_v if wpar == 0 else w12b_vv
            wreg = "A0" if wpar == 0 else "A1"
            w3buf = w3v_a if wpar == 0 else w3v_b
            w3reg = "B2" if wpar == 0 else "B3"

            widx = wii[:, ph * 24 + k_t: ph * 24 + k_t + 1]

            wflat = (biga[:, 0:8192] if wpar == 0 else None)

            def wload(e, widx=widx, wpar=wpar, which=0):
                hh = w1_h if which == 0 else w2_h
                base = 0 if wpar == 0 else 4096
                dst = biga[:, base + which * 2048: base + (which + 1) * 2048]
                return e.indirect_dma_start(
                    out=dst, out_offset=None,
                    in_=hh.ap(), in_offset=bass.IndirectOffsetOnAxis(widx, 0))
            S.add("gpsimd", lambda e, f=wload: f(e, which=0), reads=["CR", wreg],
                  writes=[wreg], is_dma=True, tag=f"p2wa{k_t}_{ph}", group="g_w12")
            S.add("gpsimd", lambda e, f=wload: f(e, which=1), reads=["CR", wreg],
                  writes=[wreg], is_dma=True, tag=f"p2wb{k_t}_{ph}", group="g_w12")

            def w3load(e, widx=widx, wpar=wpar):
                dst = bigb[:, 4096 + wpar * 2048: 4096 + (wpar + 1) * 2048]
                return e.indirect_dma_start(
                    out=dst, out_offset=None,
                    in_=w3_h.ap(), in_offset=bass.IndirectOffsetOnAxis(widx, 0))
            S.add("gpsimd", w3load, reads=["CR", w3reg], writes=[w3reg],
                  is_dma=True, tag=f"p2wc{k_t}_{ph}", group="g_w3")

            for d in range(2):
                def hgmm(e, wbuf=wbuf, d=d, which=0, bank=4):
                    last = None
                    for k in range(NCB):
                        last = e.matmul(ps[bank][:, 0:256],
                                        wbuf[:, which * 2 + d, k, :],
                                        xfm_v[:, k, :], start=(k == 0), stop=(k == 7))
                    return last
                op("tensor", lambda e, f=hgmm: f(e, which=0, bank=4),
                   reads=[wreg, "B1", "psum4"], writes=["psum4"],
                   tag=f"p2h{k_t}_{ph}_{d}")
                op("tensor", lambda e, f=hgmm: f(e, which=1, bank=5),
                   reads=[wreg, "B1", "psum5"], writes=["psum5"],
                   tag=f"p2g{k_t}_{ph}_{d}")
                op("scalar", lambda e, d=d:
                   e.activation(sil_v[:, d * 256:(d + 1) * 256], ps[4][:, 0:256], AF.Silu),
                   reads=["psum4", f"sil{d}"], writes=[f"sil{d}"], tag=f"p2s{k_t}_{ph}_{d}")
                op("vector", lambda e, d=d:
                   e.tensor_tensor(h2_vv[:, d, :], sil_v[:, d * 256:(d + 1) * 256],
                                   ps[5][:, 0:256], ALU.mult),
                   reads=[f"sil{d}", "psum5", "B0"], writes=["B0"],
                   tag=f"p2u{k_t}_{ph}_{d}")
            for cm in range(NCB):
                bank = cm // 2
                col = (cm % 2) * 256
                pr = f"psum{bank}"

                def ymm(e, w3buf=w3buf, cm=cm, bank=bank, col=col, ph=ph):
                    last = None
                    for d in range(2):
                        last = e.matmul(ps[bank][:, col:col + 256],
                                        w3buf[:, d, cm * 128:(cm + 1) * 128],
                                        h2_vv[:, d, :],
                                        start=(ph == 0 and d == 0 and cm % 2 == 0),
                                        stop=(ph == 7 and d == 1),
                                        skip_group_check=True)
                    return last
                op("tensor", ymm, reads=[w3reg, "B0", pr], writes=[pr],
                   tag=f"p2y{k_t}_{ph}_{cm}")
        yt_v = B2.rearrange("p (c t) -> p c t", t=256)
        ytm_v = B3.rearrange("p (a m) -> p a m", m=1024)
        for cm in range(NCB):
            bank = cm // 2
            col = (cm % 2) * 256
            if cm % 2 == 0:
                op("vector", lambda e, cm=cm, bank=bank, col=col:
                   e.tensor_copy(yt_v[:, cm, :], ps[bank][:, col:col + 256]),
                   reads=[f"psum{bank}", "B2"], writes=["B2"], tag=f"p2ye{k_t}_{cm}")
            else:
                op("scalar", lambda e, cm=cm, bank=bank, col=col:
                   e.copy(yt_v[:, cm, :], ps[bank][:, col:col + 256]),
                   reads=[f"psum{bank}", "B2"], writes=["B2"], tag=f"p2ye{k_t}_{cm}")
        for a in range(2):
            for cbk in range(NCB):
                transpose128(yt_v[:, cbk, a * 128:(a + 1) * 128],
                             ytm_v[:, a, cbk * 128:(cbk + 1) * 128],
                             "B2", "B3",
                             evac="vector" if cbk % 2 == 0 else "scalar")
        dma_out(ydisp_scr.ap()[k_t * 256:(k_t + 1) * 256, :].rearrange(
                    "(a p) m -> p a m", p=128),
                ytm_v[:, :, :], ["B3"], f"p2ys{k_t}", writes=[f"yd{k_t}"],
                group="g_yd")

    yd_all = [f"yd{k}" for k in range(NTILE)]
    y1g = biga[:, 0:1024]
    y2g = biga[:, 1024:2048]
    for w in range(NWIN):
        for t in range(4):
            blk = w * 4 + t
            dma_in(x1_v[:, t, :],
                   x1_scr.ap()[w * 128:(w + 1) * 128, t * 1024:(t + 1) * 1024],
                   ["x1st"], f"p2xo{w}_{t}",
                   reads=[f"x1s{p}_{w}" for p in range(2)])

            def gath(e, blk=blk, a=0, dst=None):
                return e.indirect_dma_start(
                    out=dst, out_offset=None,
                    in_=ydisp_scr.ap(),
                    in_offset=bass.IndirectOffsetOnAxis(
                        posi[:, a * 16 + blk: a * 16 + blk + 1], 0))
            S.add("gpsimd", lambda e, f=gath, blk=blk: f(e, blk, 0, y1g),
                  reads=yd_all + ["CR", "A0"], writes=["A0"], is_dma=True,
                  tag=f"p2g1{w}_{t}", group="g_ga")
            S.add("gpsimd", lambda e, f=gath, blk=blk: f(e, blk, 1, y2g),
                  reads=yd_all + ["CR", "A0"], writes=["A0"], is_dma=True,
                  tag=f"p2g2{w}_{t}", group="g_ga")
            op("vector", lambda e, blk=blk:
               e.tensor_scalar(outst[:, 0:1024], y1g, wtsb[:, blk:blk + 1], None,
                               ALU.mult),
               reads=["A0", "CR", "outst"], writes=["outst"], tag=f"p2o1{w}_{t}")
            op("vector", lambda e, blk=blk:
               e.scalar_tensor_tensor(outst[:, 0:1024], y2g,
                                      wtsb[:, 16 + blk:17 + blk],
                                      outst[:, 0:1024], ALU.mult, ALU.add),
               reads=["A0", "CR", "outst"], writes=["outst"], tag=f"p2o2{w}_{t}")
            op("vector", lambda e, t=t:
               e.tensor_tensor(outst[:, 0:1024], outst[:, 0:1024], x1_v[:, t, :],
                               ALU.add),
               reads=["x1st", "outst"], writes=["outst"], tag=f"p2o3{w}_{t}")
            dma_out(out_h.ap()[(w * 4 + t) * 128:(w * 4 + t + 1) * 128, :],
                    outst[:, 0:1024], ["outst"], f"p2ou{w}_{t}", group="g_out")
    return nc, S, dbg_outs


def finalize(nc, S):
    with nc.Block() as block:
        S.finalize(nc, block)
    return nc


def _prep_weights(inp):
    f32 = lambda a: np.asarray(a, dtype=np.float32)
    w = {}
    w["win"] = np.ascontiguousarray(
        f32(inp["L0_attn_in_proj_w"]).T.reshape(8, 128, 32, 128).transpose(2, 1, 0, 3))
    w["wout"] = np.ascontiguousarray(
        f32(inp["L0_attn_out_proj_w"]).T.reshape(16, 128, 8, 128).transpose(2, 1, 0, 3))
    w["xpwT"] = np.ascontiguousarray(
        f32(inp["L0_attn_x_proj_w"]).T.reshape(16, 128, 33).transpose(1, 0, 2))
    w["alog"] = np.ascontiguousarray(
        f32(inp["L0_attn_A_log"]).reshape(16, 128, 16).transpose(1, 0, 2))
    w["dpw"] = np.ascontiguousarray(
        f32(inp["L0_attn_dt_proj_w"])[:, 0].reshape(16, 128).T)
    w["dpb"] = np.ascontiguousarray(f32(inp["L0_attn_dt_proj_b"]).reshape(16, 128).T)
    w["dvec"] = np.ascontiguousarray(f32(inp["L0_attn_D"]).reshape(16, 128).T)
    w["cw"] = np.ascontiguousarray(
        f32(inp["L0_attn_conv_w"])[:, 0, :].reshape(16, 128, 4).transpose(1, 0, 2))
    w["cb"] = np.ascontiguousarray(f32(inp["L0_attn_conv_b"]).reshape(16, 128).T)
    w["wg"] = np.ascontiguousarray(
        f32(inp["L0_ffn_Wg"]).reshape(8, 128, 8).transpose(1, 0, 2))
    w["w1"] = np.ascontiguousarray(
        f32(inp["L0_ffn_W1"]).reshape(8, 8, 128, 8, 2, 128)
        .transpose(0, 3, 2, 4, 1, 5).reshape(8192, 2048))
    w["w2"] = np.ascontiguousarray(
        f32(inp["L0_ffn_W2"]).reshape(8, 8, 128, 8, 2, 128)
        .transpose(0, 3, 2, 4, 1, 5).reshape(8192, 2048))
    w["w3"] = np.ascontiguousarray(
        f32(inp["L0_ffn_W3"]).reshape(8, 8, 2, 128, 1024)
        .transpose(0, 1, 3, 2, 4).reshape(8192, 2048))
    w["tri"] = np.ascontiguousarray(np.triu(np.ones((128, 128), np.float32), 1))
    w["tstart"] = (np.arange(32, dtype=np.float32) * 256).reshape(1, 32)
    w["iota"] = np.arange(128, dtype=np.float32).reshape(128, 1)
    return w


def _slot_arrays(parity):
    scols = np.zeros((1, 32), dtype=np.float32)
    for j in range(NCH):
        scols[0, j] = j // 2 if (j % 2 == parity) else 4
    return np.ascontiguousarray(np.broadcast_to(scols, (128, 32)))


def kernel(**inputs):
    w = _prep_weights(inputs)
    x = np.asarray(inputs["L0_attn_x"], dtype=np.float32)
    nc, S, _ = build_program(upto="full", dbg=False)
    nc = finalize(nc, S)
    in_maps = []
    for core in range(8):
        b, parity = core // 2, core % 2
        scols = _slot_arrays(parity)
        m = {"x": np.ascontiguousarray(x[b]), "srows": None, "scols": scols}
        del m["srows"]
        m.update(w)
        in_maps.append(m)
    res = run_bass_kernel_spmd(nc, in_maps, core_ids=list(range(8)))
    out = np.empty((B, L, DM), dtype=np.float32)
    for b in range(B):
        for j in range(NCH):
            core = 2 * b + (j % 2)
            sl = j // 2
            out[b, j * TC:(j + 1) * TC] = res.results[core]["out_part"][sl * TC:(sl + 1) * TC]
    return out
